# version 01 (final)
# baseline (speedup 1.0000x reference)
import numpy as np
import ml_dtypes
from contextlib import ExitStack
import concourse.bass as bass
import concourse.mybir as mybir
from concourse.bass_utils import run_bass_kernel_spmd

F32 = mybir.dt.float32
BF16 = mybir.dt.bfloat16
AF = mybir.ActivationFunctionType
ALU = mybir.AluOpType
AX = mybir.AxisListType
NCORE = 8
BT = 16384
SEQ = 8192
D = 1024
SH = 2048
NEG = -1e30

_prog_cache = {}


class Prog:

    def __init__(self, name):
        self.nc = bass.Bass("TRN2", target_bir_lowering=False)
        self.name = name
        self.ops = []
        self.es = ExitStack()
        self.inputs = {}
        self.outputs = {}

    def din(self, name, shape, dtype=F32):
        t = self.nc.dram_tensor(name, list(shape), dtype, kind="ExternalInput")
        self.inputs[name] = shape
        return t.ap()

    def dout(self, name, shape, dtype=F32):
        t = self.nc.dram_tensor(name, list(shape), dtype, kind="ExternalOutput")
        self.outputs[name] = shape
        return t.ap()

    def sb(self, name, shape, dtype=F32):
        return self.es.enter_context(self.nc.sbuf_tensor(name, list(shape), dtype))

    def ps(self, name, shape, dtype=F32):
        return self.es.enter_context(self.nc.psum_tensor(name, list(shape), dtype))

    def op(self, eng, fn, inc=1):
        self.ops.append((eng, fn, inc))

    def dma(self, dst, src, **kw):
        self.op('sync', lambda e: e.dma_start(out=dst, in_=src, **kw), 16)

    def mm(self, out, lhsT, rhs, start=True, stop=True):
        self.op('tensor', lambda e: e.matmul(out, lhsT, rhs, start=start, stop=stop))

    def trans(self, out_ps, in_, ident):
        self.op('tensor', lambda e: e.transpose(out_ps, in_, ident))

    def act(self, out, in_, func, bias=0.0, scale=1.0):
        self.op('scalar', lambda e: e.activation(out, in_, func, bias=bias, scale=scale))

    def v(self, fname, *args, **kw):
        self.op('vector', lambda e: getattr(e, fname)(*args, **kw))

    def s(self, fname, *args, **kw):
        self.op('scalar', lambda e: getattr(e, fname)(*args, **kw))

    def ln_tok(self, xn, x, sq, col, n=D, eps=1e-5):
        self.v('reduce_sum', col[:, 0:1], x, AX.X)
        self.v('tensor_scalar_mul', col[:, 1:2], col[:, 0:1], -1.0 / n)
        self.v('tensor_scalar_add', xn, x, col[:, 1:2])
        self.act(sq, xn, AF.Square)
        self.v('reduce_sum', col[:, 0:1], sq, AX.X)
        self.v('tensor_scalar', col[:, 1:2], col[:, 0:1], 1.0 / n, eps, ALU.mult, ALU.add)
        self.act(col[:, 1:2], col[:, 1:2], AF.Ln)
        self.act(col[:, 1:2], col[:, 1:2], AF.Exp, scale=-0.5)
        self.v('tensor_scalar_mul', xn, xn, col[:, 1:2])

    def build(self):
        nc = self.nc
        pre = []
        c = 0
        for (eng, fn, inc) in self.ops:
            pre.append(c)
            c += inc
        total = c
        engs = set(e for e, _, _ in self.ops) | {'vector'}
        with self.es:
            with nc.semaphore("lad") as sem, nc.Block() as block:
                for ename in ['sync', 'tensor', 'vector', 'scalar', 'gpsimd']:
                    if ename not in engs:
                        continue
                    myops = [(pre[i], f, inc) for i, (e, f, inc) in enumerate(self.ops) if e == ename]

                    def body(eng, myops=myops, ename=ename):
                        for (p, f, inc) in myops:
                            if p > 0:
                                eng.wait_ge(sem, p)
                            f(eng).then_inc(sem, inc)
                        if ename == 'vector':
                            eng.wait_ge(sem, total)
                    getattr(block, ename)(body)
        return self.nc


def runp(progfn, key, in_maps):
    if key not in _prog_cache:
        p = progfn()
        nc = p.build()
        _prog_cache[key] = (nc, p.inputs, p.outputs)
    nc, ins, outs = _prog_cache[key]
    res = run_bass_kernel_spmd(nc, in_maps, core_ids=list(range(NCORE)))
    return res.results


def f32(a):
    return np.ascontiguousarray(a, dtype=np.float32)



def build_PO(add_ln):
    p = Prog("PO")
    ofm = p.din("ofm", (D, SH), BF16)
    W = p.din("w", (D, D))
    b = p.din("b", (D,))
    x = p.din("x", (SH, D))
    xo = p.dout("xout", (SH, D))

    w_sb = p.sb("w_sb", (128, 8 * D), BF16)
    wtmp = p.sb("wtmp", (128, D))
    ones1 = p.sb("ones1", (1, 128), BF16)
    b_sb = p.sb("b_sb", (1, D))
    b_bf = p.sb("b_bf", (1, D), BF16)
    o_sb = p.sb("o_sb", (128, 8 * 128), BF16)
    x_sb = p.sb("x_sb", (128, D))
    h_sb = p.sb("h_sb", (128, D))
    sq = p.sb("sq", (128, D))
    col = p.sb("col", (128, 2))
    out_sb = p.sb("out_sb", (128, D))
    P = p.ps("P", (128, 512))

    p.v('memset', ones1[:, :], 1.0)
    p.dma(b_sb[0:1, :], b[None, :])
    p.v('tensor_copy', b_bf[:, :], b_sb[:, :])
    for k in range(8):
        p.dma(wtmp[:, :], W[k * 128:(k + 1) * 128, :])
        p.v('tensor_copy', w_sb[:, k * D:(k + 1) * D], wtmp[:, :])

    for t in range(SH // 128):
        r = slice(t * 128, (t + 1) * 128)
        for k in range(8):
            p.dma(o_sb[:, k * 128:(k + 1) * 128], ofm[k * 128:(k + 1) * 128, r])
        p.dma(x_sb[:, :], x[r, :])
        if add_ln:
            p.ln_tok(h_sb[:, :], x_sb[:, :], sq[:, :], col)
        for n in range(2):
            ns = slice(n * 512, (n + 1) * 512)
            for k in range(8):
                p.mm(P[:, :], o_sb[:, k * 128:(k + 1) * 128],
                     w_sb[:, k * D + n * 512:k * D + n * 512 + 512],
                     start=(k == 0), stop=False)
            p.mm(P[:, :], ones1[:, :], b_bf[:, ns], start=False, stop=True)
            if add_ln:
                p.v('tensor_add', out_sb[:, ns], x_sb[:, ns], h_sb[:, ns])
                p.v('tensor_add', out_sb[:, ns], out_sb[:, ns], P[:, :])
            else:
                p.v('tensor_add', out_sb[:, ns], x_sb[:, ns], P[:, :])
            p.dma(xo[r, ns], out_sb[:, ns])
    return p


def run_PO(x, Ofm, W, b, add_ln=False):
    im = []
    for c in range(NCORE):
        s = slice(c * SH, (c + 1) * SH)
        im.append({"ofm": np.ascontiguousarray(Ofm[:, s]), "w": f32(W), "b": f32(b),
                   "x": f32(x[s])})
    res = runp(lambda: build_PO(add_ln), ("PO", add_ln), im)
    return np.concatenate([r["xout"] for r in res], axis=0)



def build_PA(rope, qmul, lnscale, perf=False):
    p = Prog("PA")
    x = p.din("x", (SH, D))
    if lnscale:
        lnw = p.din("lnw", (D,))
        lnb = p.din("lnb", (D,))
    W = p.din("wqkv", (D, 3 * D))
    bias = p.din("bias", (3 * D,))
    ident = p.din("ident", (128, 128))
    if rope:
        tabs = {nm: p.din(nm, (SH, D)) for nm in ("cq", "sq_", "ck", "sk")}
    qfm = p.dout("qfm", (D, SH), BF16)
    kfm = p.dout("kfm", (D, SH), BF16)
    vo = p.dout("v", (SH, D), BF16)
    if perf:
        ssk = p.dout("ssk", (SH, 16))
        minss = p.dout("minss", (1, 1))
        ss_sb = p.sb("ss_sb", (128, 16))
        ssmin = p.sb("ssmin", (128, 1))
        ssr = p.sb("ssr", (1, 128))
        SSP = p.ps("SSP", (1, 128))

    id_bf = p.sb("id_bf", (128, 128), BF16)
    id_f = p.sb("id_f", (128, 128))
    w_sb = p.sb("w_sb", (128, 8 * 3 * D), BF16)
    wtmp = p.sb("wtmp", (128, 3 * D))
    ones1 = p.sb("ones1", (1, 128), BF16)
    b_sb = p.sb("b_sb", (1, 3 * D))
    b_bf = p.sb("b_bf", (1, 3 * D), BF16)
    x_sb = p.sb("x_sb", (128, D))
    xn = p.sb("xn", (128, D))
    sq = p.sb("sq", (128, D))
    col = p.sb("col", (128, 2))
    xbf = p.sb("xbf", (128, D), BF16)
    xfm = p.sb("xfm", (128, 8 * 128), BF16)
    tq = p.sb("tq", (128, D))
    ts = p.sb("ts_", (128, D))
    qbf = p.sb("qbf", (128, D), BF16)
    tab_sb = p.sb("tab_sb", (128, 4 * D))
    TP = p.ps("TP", (128, 128), BF16)
    QKV = p.ps("QKV", (128, 3 * D))

    p.v('memset', ones1[:, :], 1.0)
    p.dma(id_f[:, :], ident[:, :])
    p.v('tensor_copy', id_bf[:, :], id_f[:, :])
    if lnscale:
        lnw_b = p.sb("lnw_b", (128, D))
        lnb_b = p.sb("lnb_b", (128, D))
        lrow = p.sb("lrow", (1, 2 * D))
        lrow_bf = p.sb("lrow_bf", (1, 2 * D), BF16)
        p.dma(lrow[0:1, 0:D], lnw[None, :])
        p.dma(lrow[0:1, D:2 * D], lnb[None, :])
        p.v('tensor_copy', lrow_bf[:, :], lrow[:, :])
        for n in range(4):
            p.mm(QKV[:, 0:512], ones1[:, :], lrow_bf[:, n * 512:(n + 1) * 512])
            dstb = lnw_b if n < 2 else lnb_b
            p.v('tensor_copy', dstb[:, (n % 2) * 512:(n % 2) * 512 + 512], QKV[:, 0:512])
    p.dma(b_sb[0:1, :], bias[None, :])
    p.v('tensor_copy', b_bf[:, :], b_sb[:, :])
    for k in range(8):
        p.dma(wtmp[:, :], W[k * 128:(k + 1) * 128, :])
        p.v('tensor_copy', w_sb[:, k * 3 * D:(k + 1) * 3 * D], wtmp[:, :])
    if perf:
        p.v('memset', ssmin[:, :], 1e30)

    def swap_halves(dst, src):
        sa = src.rearrange("p (h two s) -> p h two s", two=2, s=32)
        da = dst.rearrange("p (h two s) -> p h two s", two=2, s=32)
        p.v('tensor_copy', da[:, :, 0, :], sa[:, :, 1, :])
        p.v('tensor_copy', da[:, :, 1, :], sa[:, :, 0, :])

    for t in range(SH // 128):
        r = slice(t * 128, (t + 1) * 128)
        p.dma(x_sb[:, :], x[r, :])
        p.ln_tok(xn[:, :], x_sb[:, :], sq[:, :], col)
        if lnscale:
            p.v('tensor_mul', xn[:, :], xn[:, :], lnw_b[:, :])
            p.v('tensor_add', xn[:, :], xn[:, :], lnb_b[:, :])
        p.v('tensor_copy', xbf[:, :], xn[:, :])
        for k in range(8):
            p.trans(TP[:, :], xbf[:, k * 128:(k + 1) * 128], id_bf[:, :])
            p.v('tensor_copy', xfm[:, k * 128:(k + 1) * 128], TP[:, :])
        for n in range(6):
            ns = slice(n * 512, (n + 1) * 512)
            for k in range(8):
                p.mm(QKV[:, ns], xfm[:, k * 128:(k + 1) * 128],
                     w_sb[:, k * 3 * D + n * 512:k * 3 * D + n * 512 + 512],
                     start=(k == 0), stop=False)
            p.mm(QKV[:, ns], ones1[:, :], b_bf[:, ns], start=False, stop=True)
        if rope:
            for i, nm in enumerate(("cq", "sq_", "ck", "sk")):
                p.dma(tab_sb[:, i * D:(i + 1) * D], tabs[nm][r, :])
        for j in range(2):
            qs = slice(j * D, (j + 1) * D)
            if rope:
                p.v('tensor_mul', tq[:, :], tab_sb[:, 2 * j * D:(2 * j + 1) * D], QKV[:, qs])
                swap_halves(ts[:, :], QKV[:, qs])
                p.v('tensor_mul', ts[:, :], ts[:, :], tab_sb[:, (2 * j + 1) * D:(2 * j + 2) * D])
                p.v('tensor_add', tq[:, :], tq[:, :], ts[:, :])
                p.v('tensor_copy', qbf[:, :], tq[:, :])
            else:
                p.s('mul', qbf[:, :], QKV[:, qs], qmul if j == 0 else 1.0)
            dst = qfm if j == 0 else kfm
            for k in range(8):
                p.trans(TP[:, :], qbf[:, k * 128:(k + 1) * 128], id_bf[:, :])
                p.v('tensor_copy', xbf[:, k * 128:(k + 1) * 128], TP[:, :])
                p.dma(dst[k * 128:(k + 1) * 128, r], xbf[:, k * 128:(k + 1) * 128])
        p.v('tensor_copy', qbf[:, :], QKV[:, 2 * D:3 * D])
        p.dma(vo[r, :], qbf[:, :])
        if perf:
            for hh in range(16):
                p.act(sq[:, 0:64], QKV[:, D + hh * 64:D + (hh + 1) * 64], AF.Square)
                p.v('reduce_sum', ss_sb[:, hh:hh + 1], sq[:, 0:64], AX.X)
            p.dma(ssk[r, :], ss_sb[:, :])
            for hh in range(16):
                p.v('tensor_tensor', ssmin[:, 0:1], ssmin[:, 0:1], ss_sb[:, hh:hh + 1], ALU.min)
    if perf:
        p.v('tensor_copy', xbf[:, 0:1], ssmin[:, 0:1])
        p.trans(TP[0:1, 0:128], xbf[:, 0:1], id_bf[:, :])
        p.v('tensor_copy', ssr[0:1, :], TP[0:1, 0:128])
        p.v('tensor_reduce', ssr[0:1, 0:1], ssr[0:1, :], AX.X, ALU.min)
        p.dma(minss[0:1, :], ssr[0:1, 0:1])
    return p


def run_PA(x, Wqkv, bias, rope_tabs=None, qmul=1.0, lnscale=None, perf=False):
    ident = np.eye(128, dtype=np.float32)
    rope = rope_tabs is not None
    im = []
    for c in range(NCORE):
        s = slice(c * SH, (c + 1) * SH)
        d = {"x": f32(x[s]), "wqkv": f32(Wqkv), "bias": f32(bias), "ident": ident}
        if lnscale is not None:
            d["lnw"] = f32(lnscale[0]); d["lnb"] = f32(lnscale[1])
        if rope:
            for nm in ("cq", "sq_", "ck", "sk"):
                d[nm] = f32(rope_tabs[nm][s])
        im.append(d)
    res = runp(lambda: build_PA(rope, qmul, lnscale is not None, perf),
               ("PA", rope, qmul, lnscale is not None, perf), im)
    qfm = np.concatenate([r["qfm"] for r in res], axis=1)
    kfm = np.concatenate([r["kfm"] for r in res], axis=1)
    v = np.concatenate([r["v"] for r in res], axis=0)
    if perf:
        ssk = np.concatenate([r["ssk"] for r in res], axis=0)
        minss = np.stack([r["minss"].reshape(()) for r in res]).reshape(8, 1)
        return qfm, kfm, v, ssk, minss
    return qfm, kfm, v



def build_PB(causal):
    p = Prog("PB")
    NQ = SEQ // 512
    NK = SEQ // 128
    qfm = p.din("qfm", (256, SEQ), BF16)
    kfm = p.din("kfm", (256, SEQ), BF16)
    vin = p.din("v", (SEQ, 256), BF16)
    if causal:
        masks = p.din("masks", (512, 512))
    ofm = p.dout("ofm", (256, SEQ), BF16)

    q_sb = p.sb("q_sb", (64, SEQ), BF16)
    k_sb = p.sb("k_sb", (64, SEQ), BF16)
    v65 = p.sb("v65", (128, 65 * NK), BF16)
    m_sb = p.sb("m_sb", (128, 4 * 512))
    expS = p.sb("expS", (128, 512), BF16)
    dinv = p.sb("dinv", (1, 512))
    dinv_bf = p.sb("dinv_bf", (1, 512), BF16)
    ones1 = p.sb("ones1", (1, 64), BF16)
    o_sb = p.sb("o_sb", (64, 512))
    o_bf = p.sb("o_bf", (64, 512), BF16)
    S = p.ps("S", (128, 512))
    O = p.ps("O", (65, 512))
    B = p.ps("B", (64, 512))

    p.v('memset', ones1[:, :], 1.0)
    if causal:
        for d_ in range(4):
            p.dma(m_sb[:, d_ * 512:(d_ + 1) * 512], masks[d_ * 128:(d_ + 1) * 128, :])
    for h in range(4):
        p.dma(q_sb[:, :], qfm[h * 64:(h + 1) * 64, :])
        p.dma(k_sb[:, :], kfm[h * 64:(h + 1) * 64, :])
        p.v('memset', v65[:, :], 1.0)
        for o_ in range(NK):
            p.dma(v65[:, o_ * 65:o_ * 65 + 64],
                  vin[o_ * 128:(o_ + 1) * 128, h * 64:(h + 1) * 64])
        for qc in range(NQ):
            kmax = 4 * qc + 4 if causal else NK
            qs = slice(qc * 512, (qc + 1) * 512)
            for kb in range(kmax):
                p.mm(S[:, :], k_sb[:, kb * 128:(kb + 1) * 128], q_sb[:, qs])
                if causal and kb >= 4 * qc:
                    d_ = kb - 4 * qc
                    p.v('tensor_add', S[:, :], m_sb[:, d_ * 512:(d_ + 1) * 512], S[:, :])
                p.act(expS[:, :], S[:, :], AF.Exp)
                p.mm(O[:, :], v65[:, kb * 65:(kb + 1) * 65], expS[:, :],
                     start=(kb == 0), stop=(kb == kmax - 1))
            p.v('reciprocal', dinv[:, :], O[64:65, :])
            p.v('tensor_copy', dinv_bf[:, :], dinv[:, :])
            p.mm(B[:, :], ones1[:, :], dinv_bf[:, :])
            p.v('tensor_copy', o_sb[:, :], O[0:64, :])
            p.v('tensor_mul', o_sb[:, :], o_sb[:, :], B[:, :])
            p.v('tensor_copy', o_bf[:, :], o_sb[:, :])
            p.dma(ofm[h * 64:(h + 1) * 64, qs], o_bf[:, :])
    return p


def causal_masks():
    m = np.zeros((512, 512), dtype=np.float32)
    for d_ in range(4):
        for r_ in range(128):
            j = d_ * 128 + r_
            m[d_ * 128 + r_, :] = np.where(np.arange(512) >= j, 0.0, NEG)
    return m


def run_PB(qfm, kfm, v, causal):
    mk = causal_masks()
    im = []
    for c in range(NCORE):
        b_, hg = c // 4, c % 4
        rs = slice(hg * 256, (hg + 1) * 256)
        cs = slice(b_ * SEQ, (b_ + 1) * SEQ)
        d = {"qfm": np.ascontiguousarray(qfm[rs, cs]),
             "kfm": np.ascontiguousarray(kfm[rs, cs]),
             "v": np.ascontiguousarray(v[cs, rs])}
        if causal:
            d["masks"] = mk
        im.append(d)
    res = runp(lambda: build_PB(causal), ("PB", causal), im)
    Ofm = np.zeros((D, BT), dtype=ml_dtypes.bfloat16)
    for c in range(NCORE):
        b_, hg = c // 4, c % 4
        Ofm[hg * 256:(hg + 1) * 256, b_ * SEQ:(b_ + 1) * SEQ] = res[c]["ofm"]
    return Ofm



def block_attn_b(x, ins):
    qfm, kfm, v = run_PA(x, ins['L0_attn_b_W_qkv'], np.zeros(3 * D, np.float32), qmul=0.125)
    Ofm = run_PB(qfm, kfm, v, causal=True)
    return run_PO(x, Ofm, ins['L0_attn_b_W_out'], np.zeros(D, np.float32), add_ln=False)


def _perm_eo():
    p = []
    for h in range(16):
        p += [h * 64 + i for i in range(0, 64, 2)]
        p += [h * 64 + i for i in range(1, 64, 2)]
    return np.array(p)


def block_attn_a_L1(x, ins):
    perm = _perm_eo()
    Wq = ins['L1_attn_a_Wq'].astype(np.float32).T[:, perm]
    Wk = ins['L1_attn_a_Wk'].astype(np.float32).T[:, perm]
    Wv = ins['L1_attn_a_Wv'].astype(np.float32).T
    bq = ins['L1_attn_a_bq'].astype(np.float32)[perm]
    bk = ins['L1_attn_a_bk'].astype(np.float32)[perm]
    bv = ins['L1_attn_a_bv'].astype(np.float32)
    Wqkv = np.concatenate([Wq, Wk, Wv], axis=1)
    bias = np.concatenate([bq, bk, bv])
    hd = 64
    inv = 1.0 / 10000.0 ** (np.arange(0, hd, 2, dtype=np.float32) / hd)
    t = np.arange(SEQ, dtype=np.float32)
    f = np.outer(t, inv)
    cos, sin = np.cos(f), np.sin(f)
    chead = np.concatenate([cos, cos], axis=1)
    shead = np.concatenate([-sin, sin], axis=1)
    cfull = np.tile(chead, (2, 16))
    sfull = np.tile(shead, (2, 16))
    tabs = {"cq": cfull * 0.125, "sq_": sfull * 0.125, "ck": cfull, "sk": sfull}
    qfm, kfm, v = run_PA(x, Wqkv, bias, rope_tabs=tabs, qmul=1.0,
                         lnscale=(ins['L1_attn_a_ln_w'], ins['L1_attn_a_ln_b']))
    Ofm = run_PB(qfm, kfm, v, causal=False)
    return run_PO(x, Ofm, ins['L1_attn_a_Wo'].astype(np.float32).T,
                  ins['L1_attn_a_bo'], add_ln=True)



def build_P6():
    I = 2048
    E = 8
    p = Prog("P6")
    x = p.din("x", (SH, D))
    lnw = p.din("lnw", (D,))
    lnb = p.din("lnb", (D,))
    rw = p.din("router_w", (D, E))
    gw = p.din("gate_w", (E * D, I))
    uw = p.din("up_w", (E * D, I))
    dw = p.din("down_w", (E * I, D))
    sg = p.din("sh_gate", (D, I))
    su = p.din("sh_up", (D, I))
    sd = p.din("sh_down", (I, D))
    oh = p.din("onehot", (E, E * 128))
    ident = p.din("ident", (128, 128))
    xo = p.dout("xout", (SH, D))

    id_f = p.sb("id_f", (128, 128))
    id_bf = p.sb("id_bf", (128, 128), BF16)
    xn_fm = p.sb("xn_fm", (128, 8 * SH), BF16)
    lwcol = p.sb("lwcol", (128, 16))
    x_sb = p.sb("x_sb", (128, D))
    t1 = p.sb("t1", (128, D))
    t2 = p.sb("t2", (128, D))
    sq = p.sb("sq", (128, D))
    col = p.sb("col", (128, 2))
    xbf = p.sb("xbf", (128, D), BF16)
    rwbf = p.sb("rwbf", (128, 8 * E), BF16)
    oh_bf = p.sb("oh_bf", (8, E * 128), BF16)
    wT_bf = p.sb("wT_bf", (8, SH), BF16)
    wtmp = p.sb("wtmp", (128, I))
    gwbf = p.sb("gwbf", (128, 8 * I), BF16)
    uwbf = p.sb("uwbf", (128, 8 * I), BF16)
    dwtmp = p.sb("dwtmp", (128, 512))
    dwbf = p.sb("dwbf", (128, 512), BF16)
    xe_sb = p.sb("xe_sb", (128, 8 * 512), BF16)
    gel = p.sb("gel", (128, 512), BF16)
    hmid = p.sb("hmid", (128, 16 * 512), BF16)
    bcw_bf = p.sb("bcw_bf", (128, 512), BF16)
    acc = p.sb("acc", (128, 4 * D))
    out_sb = p.sb("out_sb", (128, D))
    TP = p.ps("TP", (128, 128), BF16)
    PS1 = p.ps("PS1", (128, 512))
    PS2 = p.ps("PS2", (128, 512))
    PS3 = p.ps("PS3", (128, 512))
    DP = [p.ps(f"DP{i}", (128, 512)) for i in range(4)]

    NT = SH // 128
    p.dma(id_f[:, :], ident[:, :])
    p.v('tensor_copy', id_bf[:, :], id_f[:, :])
    p.dma(lwcol[:, 0:8], lnw.rearrange("(k p) -> p k", p=128), allow_slow_non_contiguous=True)
    p.dma(lwcol[:, 8:16], lnb.rearrange("(k p) -> p k", p=128), allow_slow_non_contiguous=True)
    for k in range(8):
        p.dma(wtmp[:, k * E:(k + 1) * E], rw[k * 128:(k + 1) * 128, :])
    p.v('tensor_copy', rwbf[:, :], wtmp[:, 0:E * 8])
    p.dma(wtmp[0:8, 0:E * 128], oh[:, :])
    p.v('tensor_copy', oh_bf[:, :], wtmp[0:8, 0:E * 128])

    for t in range(NT):
        r = slice(t * 128, (t + 1) * 128)
        p.dma(x_sb[:, :], x[r, :])
        p.ln_tok(t1[:, :], x_sb[:, :], sq[:, :], col)
        p.ln_tok(t2[:, :], t1[:, :], sq[:, :], col)
        p.v('tensor_copy', xbf[:, :], t2[:, :])
        for k in range(8):
            p.trans(TP[:, :], xbf[:, k * 128:(k + 1) * 128], id_bf[:, :])
            p.v('tensor_scalar', xn_fm[:, k * SH + t * 128:k * SH + t * 128 + 128],
                TP[:, :], lwcol[:, k:k + 1], lwcol[:, 8 + k:9 + k], ALU.mult, ALU.add)
        for k in range(8):
            p.mm(PS1[:, 0:E], xn_fm[:, k * SH + t * 128:k * SH + t * 128 + 128],
                 rwbf[:, k * E:(k + 1) * E], start=(k == 0), stop=(k == 7))
        p.v('reduce_max', col[:, 0:1], PS1[:, 0:E], AX.X)
        p.act(col[:, 1:2], col[:, 0:1], AF.Sigmoid)
        for e in range(E):
            p.v('tensor_tensor', t1[:, e:e + 1], PS1[:, e:e + 1], col[:, 0:1], ALU.is_ge)
        p.v('tensor_scalar_mul', t1[:, 0:E], t1[:, 0:E], col[:, 1:2])
        p.v('tensor_copy', xbf[:, 0:E], t1[:, 0:E])
        p.trans(TP[0:E, 0:128], xbf[:, 0:E], id_bf[:, :])
        p.v('tensor_copy', wT_bf[:, t * 128:(t + 1) * 128], TP[0:E, 0:128])

    for ch in range(4):
        cs = slice(ch * 512, (ch + 1) * 512)
        p.v('memset', acc[:, :], 0.0)
        for e in range(E + 1):
            shared = (e == E)
            gsrc = sg if shared else gw
            usrc = su if shared else uw
            off = 0 if shared else e * D
            for k in range(8):
                p.dma(wtmp[:, :], gsrc[off + k * 128:off + (k + 1) * 128, :])
                p.v('tensor_copy', gwbf[:, k * I:(k + 1) * I], wtmp[:, :])
                p.dma(wtmp[:, :], usrc[off + k * 128:off + (k + 1) * 128, :])
                p.v('tensor_copy', uwbf[:, k * I:(k + 1) * I], wtmp[:, :])
            if shared:
                for k in range(8):
                    p.v('tensor_copy', xe_sb[:, k * 512:(k + 1) * 512],
                        xn_fm[:, k * SH + ch * 512:k * SH + ch * 512 + 512])
            else:
                p.mm(PS3[:, :], oh_bf[:, e * 128:(e + 1) * 128], wT_bf[:, cs])
                p.v('tensor_copy', bcw_bf[:, :], PS3[:, :])
                for k in range(8):
                    p.v('tensor_mul', xe_sb[:, k * 512:(k + 1) * 512],
                        xn_fm[:, k * SH + ch * 512:k * SH + ch * 512 + 512], bcw_bf[:, :])
            for m in range(16):
                for k in range(8):
                    p.mm(PS1[:, :], gwbf[:, k * I + m * 128:k * I + m * 128 + 128],
                         xe_sb[:, k * 512:(k + 1) * 512], start=(k == 0), stop=(k == 7))
                for k in range(8):
                    p.mm(PS2[:, :], uwbf[:, k * I + m * 128:k * I + m * 128 + 128],
                         xe_sb[:, k * 512:(k + 1) * 512], start=(k == 0), stop=(k == 7))
                p.act(gel[:, :], PS1[:, :], AF.Silu)
                p.v('tensor_mul', hmid[:, m * 512:(m + 1) * 512], gel[:, :], PS2[:, :])
            dsrc = sd if shared else dw
            doff = 0 if shared else e * I
            for n in range(2):
                for k in range(16):
                    p.dma(dwtmp[:, :], dsrc[doff + k * 128:doff + (k + 1) * 128, n * 512:(n + 1) * 512])
                    p.v('tensor_copy', dwbf[:, :], dwtmp[:, :])
                    for mt in range(4):
                        p.mm(DP[mt][:, :],
                             hmid[:, k * 512 + mt * 128:k * 512 + mt * 128 + 128],
                             dwbf[:, :], start=(k == 0), stop=(k == 15))
                for mt in range(4):
                    p.v('tensor_add', acc[:, mt * D + n * 512:mt * D + n * 512 + 512],
                        acc[:, mt * D + n * 512:mt * D + n * 512 + 512], DP[mt][:, :])
        for mt in range(4):
            t = ch * 4 + mt
            r = slice(t * 128, (t + 1) * 128)
            p.dma(x_sb[:, :], x[r, :])
            p.ln_tok(t1[:, :], x_sb[:, :], sq[:, :], col)
            p.v('tensor_add', t2[:, :], x_sb[:, :], t1[:, :])
            p.v('tensor_add', out_sb[:, :], t2[:, :], acc[:, mt * D:(mt + 1) * D])
            p.dma(xo[r, :], out_sb[:, :])
    return p


def block_moe(x, ins, pfx):
    E = 8
    oh = np.zeros((E, E * 128), np.float32)
    for e in range(E):
        oh[e, e * 128:(e + 1) * 128] = 1.0
    ident = np.eye(128, dtype=np.float32)
    im = []
    for c in range(NCORE):
        s = slice(c * SH, (c + 1) * SH)
        im.append({
            "x": f32(x[s]),
            "lnw": f32(ins[pfx + 'ln_w']), "lnb": f32(ins[pfx + 'ln_b']),
            "router_w": f32(ins[pfx + 'router_w']),
            "gate_w": f32(ins[pfx + 'gate_w']).reshape(E * D, 2048),
            "up_w": f32(ins[pfx + 'up_w']).reshape(E * D, 2048),
            "down_w": f32(ins[pfx + 'down_w']).reshape(E * 2048, D),
            "sh_gate": f32(ins[pfx + 'sh_gate']), "sh_up": f32(ins[pfx + 'sh_up']),
            "sh_down": f32(ins[pfx + 'sh_down']),
            "onehot": oh, "ident": ident,
        })
    res = runp(build_P6, ("P6",), im)
    return np.concatenate([r["xout"] for r in res], axis=0)



def build_PLN():
    p = Prog("PLN")
    x = p.din("x", (SH, D))
    yo = p.dout("y", (SH, D))
    x_sb = p.sb("x_sb", (128, D))
    yn = p.sb("yn", (128, D))
    sq = p.sb("sq", (128, D))
    col = p.sb("col", (128, 2))
    for t in range(SH // 128):
        r = slice(t * 128, (t + 1) * 128)
        p.dma(x_sb[:, :], x[r, :])
        p.ln_tok(yn[:, :], x_sb[:, :], sq[:, :], col)
        p.dma(yo[r, :], yn[:, :])
    return p


def run_PLN(x):
    im = [{"x": f32(x[c * SH:(c + 1) * SH])} for c in range(NCORE)]
    res = runp(build_PLN, ("PLN",), im)
    return np.concatenate([r["y"] for r in res], axis=0)



def build_P2a():
    IN = 512
    p = Prog("P2a")
    y = p.din("y", (SH, D))
    ym1 = p.din("ym1", (1, D))
    W = p.din("wqkv", (D, 3 * IN))
    qsc = p.din("qsc", (64,))
    ksc = p.din("ksc", (64,))
    tq_t = p.din("tabq", (SH, 2 * IN))
    tk_t = p.din("tabk", (SH, 2 * IN))
    ident = p.din("ident", (128, 128))
    qfm = p.dout("qfm", (IN, SH), BF16)
    kfm = p.dout("kfm", (IN, SH), BF16)
    vo = p.dout("v", (SH, IN), BF16)

    id_f = p.sb("id_f", (128, 128))
    id_bf = p.sb("id_bf", (128, 128), BF16)
    w_sb = p.sb("w_sb", (128, 8 * 3 * IN), BF16)
    wtmp = p.sb("wtmp", (128, 3 * IN))
    ones1 = p.sb("ones1", (1, 128), BF16)
    scrow = p.sb("scrow", (1, 2 * IN))
    scrow_bf = p.sb("scrow_bf", (1, 2 * IN), BF16)
    qsc_sb = p.sb("qsc_sb", (128, 2 * IN))
    h_sb = p.sb("h_sb", (128, D))
    hbf = p.sb("hbf", (128, D), BF16)
    hfm = p.sb("hfm", (128, 8 * 128), BF16)
    sq64 = p.sb("sq64", (128, 64))
    col = p.sb("col", (128, 2))
    qn = p.sb("qn", (128, IN))
    tab_sb = p.sb("tab_sb", (128, 2 * IN))
    tqv = p.sb("tqv", (128, IN))
    tsv = p.sb("tsv", (128, IN))
    qbf = p.sb("qbf", (128, IN), BF16)
    TP = p.ps("TP", (128, 128), BF16)
    QKV = p.ps("QKV", (128, 3 * IN))
    BCp = p.ps("BCp", (128, 512))

    p.v('memset', ones1[:, :], 1.0)
    p.dma(id_f[:, :], ident[:, :])
    p.v('tensor_copy', id_bf[:, :], id_f[:, :])
    for k in range(8):
        p.dma(wtmp[:, :], W[k * 128:(k + 1) * 128, :])
        p.v('tensor_copy', w_sb[:, k * 3 * IN:(k + 1) * 3 * IN], wtmp[:, :])
    p.dma(scrow[0:1, 0:64], qsc[None, :])
    p.dma(scrow[0:1, IN:IN + 64], ksc[None, :])
    for hh in range(1, 8):
        p.v('tensor_copy', scrow[0:1, hh * 64:(hh + 1) * 64], scrow[0:1, 0:64])
        p.v('tensor_copy', scrow[0:1, IN + hh * 64:IN + (hh + 1) * 64], scrow[0:1, IN:IN + 64])
    p.v('tensor_copy', scrow_bf[:, :], scrow[:, :])
    for n in range(2):
        p.mm(BCp[:, :], ones1[:, :], scrow_bf[:, n * 512:(n + 1) * 512])
        p.v('tensor_copy', qsc_sb[:, n * 512:(n + 1) * 512], BCp[:, :])

    def swap32(dst, src):
        sa = src.rearrange("p (h two s) -> p h two s", two=2, s=32)
        da = dst.rearrange("p (h two s) -> p h two s", two=2, s=32)
        p.v('tensor_copy', da[:, :, 0, :], sa[:, :, 1, :])
        p.v('tensor_copy', da[:, :, 1, :], sa[:, :, 0, :])

    for t in range(SH // 128):
        r = slice(t * 128, (t + 1) * 128)
        p.dma(h_sb[:, 0:512], y[r, 0:512])
        if t == 0:
            p.dma(h_sb[0:1, 512:D], ym1[0:1, 512:D])
            p.dma(h_sb[1:128, 512:D], y[0:127, 512:D])
        else:
            p.dma(h_sb[:, 512:D], y[t * 128 - 1:t * 128 + 127, 512:D])
        p.v('tensor_copy', hbf[:, :], h_sb[:, :])
        for k in range(8):
            p.trans(TP[:, :], hbf[:, k * 128:(k + 1) * 128], id_bf[:, :])
            p.v('tensor_copy', hfm[:, k * 128:(k + 1) * 128], TP[:, :])
        for n in range(3):
            ns = slice(n * 512, (n + 1) * 512)
            for k in range(8):
                p.mm(QKV[:, ns], hfm[:, k * 128:(k + 1) * 128],
                     w_sb[:, k * 3 * IN + n * 512:k * 3 * IN + n * 512 + 512],
                     start=(k == 0), stop=(k == 7))
        for j in range(2):
            base = j * IN
            tsrc = tq_t if j == 0 else tk_t
            for hh in range(8):
                hsl = slice(base + hh * 64, base + (hh + 1) * 64)
                p.act(sq64[:, :], QKV[:, hsl], AF.Square)
                p.v('reduce_sum', col[:, 0:1], sq64[:, :], AX.X)
                p.s('sqrt', col[:, 1:2], col[:, 0:1])
                p.v('tensor_scalar_max', col[:, 1:2], col[:, 1:2], 1e-12)
                p.v('reciprocal', col[:, 1:2], col[:, 1:2])
                p.v('tensor_scalar_mul', qn[:, hh * 64:(hh + 1) * 64], QKV[:, hsl], col[:, 1:2])
            p.v('tensor_mul', qn[:, :], qn[:, :], qsc_sb[:, j * 512:(j + 1) * 512])
            p.dma(tab_sb[:, :], tsrc[r, :])
            p.v('tensor_mul', tqv[:, :], qn[:, :], tab_sb[:, 0:IN])
            swap32(tsv[:, :], qn[:, :])
            p.v('tensor_mul', tsv[:, :], tsv[:, :], tab_sb[:, IN:2 * IN])
            p.v('tensor_add', tqv[:, :], tqv[:, :], tsv[:, :])
            p.v('tensor_copy', qbf[:, :], tqv[:, :])
            dst = qfm if j == 0 else kfm
            for k in range(4):
                p.trans(TP[:, :], qbf[:, k * 128:(k + 1) * 128], id_bf[:, :])
                p.v('tensor_copy', hfm[:, k * 128:(k + 1) * 128], TP[:, :])
                p.dma(dst[k * 128:(k + 1) * 128, r], hfm[:, k * 128:(k + 1) * 128])
        p.v('tensor_copy', qbf[:, :], QKV[:, 2 * IN:3 * IN])
        p.dma(vo[r, :], qbf[:, :])
    return p



def build_P2b():
    IN = 512
    p = Prog("P2b")
    qfm = p.din("qfm", (IN, SH), BF16)
    kfm = p.din("kfm", (IN, SH + 512), BF16)
    vin = p.din("v", (SH + 512, IN), BF16)
    mreg = p.din("mreg", (1024, 512))
    mw0 = p.din("mw0", (1024, 512))
    ofm = p.dout("ofm", (IN, SH), BF16)

    m_sb = p.sb("m_sb", (128, 16 * 512))
    qsb = p.sb("qsb", (128, 4 * SH), BF16)
    ksb = p.sb("ksb", (128, 4 * (SH + 512)), BF16)
    v65 = p.sb("v65", (128, 8 * 65), BF16)
    expS = p.sb("expS", (128, 512), BF16)
    dinv = p.sb("dinv", (1, 512))
    dinv_bf = p.sb("dinv_bf", (1, 512), BF16)
    ones1 = p.sb("ones1", (1, 64), BF16)
    o_sb = p.sb("o_sb", (64, 512))
    o_bf = p.sb("o_bf", (64, 512), BF16)
    S = p.ps("S", (128, 512))
    O = p.ps("O", (65, 512))
    B = p.ps("B", (64, 512))

    p.v('memset', ones1[:, :], 1.0)
    for kb in range(8):
        p.dma(m_sb[:, kb * 512:(kb + 1) * 512], mreg[kb * 128:(kb + 1) * 128, :])
        p.dma(m_sb[:, (8 + kb) * 512:(9 + kb) * 512], mw0[kb * 128:(kb + 1) * 128, :])
    for pt in range(4):
        p.dma(qsb[:, pt * SH:(pt + 1) * SH], qfm[pt * 128:(pt + 1) * 128, :])
        p.dma(ksb[:, pt * (SH + 512):(pt + 1) * (SH + 512)], kfm[pt * 128:(pt + 1) * 128, :])
    for w in range(4):
        for h in range(8):
            p.v('memset', v65[:, :], 1.0)
            for kb in range(8):
                p.dma(v65[:, kb * 65:kb * 65 + 64],
                      vin[w * 512 + kb * 128:w * 512 + (kb + 1) * 128, h * 64:(h + 1) * 64])
            pt, po = h // 2, (h % 2) * 64
            for kb in range(8):
                kcol = pt * (SH + 512) + w * 512 + kb * 128
                qcol = pt * SH + w * 512
                p.mm(S[:, :], ksb[po:po + 64, kcol:kcol + 128],
                     qsb[po:po + 64, qcol:qcol + 512])
                moff = ((8 if w == 0 else 0) + kb) * 512
                p.v('tensor_add', S[:, :], m_sb[:, moff:moff + 512], S[:, :])
                p.act(expS[:, :], S[:, :], AF.Exp)
                p.mm(O[:, :], v65[:, kb * 65:(kb + 1) * 65], expS[:, :],
                     start=(kb == 0), stop=(kb == 7))
            p.v('reciprocal', dinv[:, :], O[64:65, :])
            p.v('tensor_copy', dinv_bf[:, :], dinv[:, :])
            p.mm(B[:, :], ones1[:, :], dinv_bf[:, :])
            p.v('tensor_copy', o_sb[:, :], O[0:64, :])
            p.v('tensor_mul', o_sb[:, :], o_sb[:, :], B[:, :])
            p.v('tensor_copy', o_bf[:, :], o_sb[:, :])
            p.dma(ofm[h * 64:(h + 1) * 64, w * 512:(w + 1) * 512], o_bf[:, :])
    return p



def build_P2c1():
    IN = 512
    p = Prog("P2c1")
    y = p.din("y", (SH + 128, D))
    ofm = p.din("ofm", (IN, SH + 128), BF16)
    wo = p.din("wo", (IN, D))
    yp = p.dout("yp", (SH + 128, D))

    wobf = p.sb("wobf", (128, 4 * D), BF16)
    wtmp = p.sb("wtmp", (128, D))
    o_sb = p.sb("o_sb", (128, 4 * 128), BF16)
    y_sb = p.sb("y_sb", (128, D))
    out_sb = p.sb("out_sb", (128, D))
    P = p.ps("P", (128, 512))

    for k in range(4):
        p.dma(wtmp[:, :], wo[k * 128:(k + 1) * 128, :])
        p.v('tensor_copy', wobf[:, k * D:(k + 1) * D], wtmp[:, :])
    for t in range(17):
        r = slice(t * 128, (t + 1) * 128)
        for k in range(4):
            p.dma(o_sb[:, k * 128:(k + 1) * 128], ofm[k * 128:(k + 1) * 128, r])
        p.dma(y_sb[:, :], y[r, :])
        for n in range(2):
            ns = slice(n * 512, (n + 1) * 512)
            for k in range(4):
                p.mm(P[:, :], o_sb[:, k * 128:(k + 1) * 128],
                     wobf[:, k * D + n * 512:k * D + n * 512 + 512],
                     start=(k == 0), stop=(k == 3))
            p.v('tensor_add', out_sb[:, ns], y_sb[:, ns], P[:, :])
            p.dma(yp[r, ns], out_sb[:, ns])
    return p



def build_P2c2(addx):
    W1C = 5632
    KD = 22
    p = Prog("P2c2")
    yp = p.din("yp", (SH + 1, D))
    w1 = p.din("w1re", (D, W1C))
    b1 = p.din("b1re", (W1C,))
    w2 = p.din("w2p", (2816, D))
    b2 = p.din("b2", (D,))
    lg = p.din("lg", (D,))
    lb = p.din("lb", (D,))
    ident = p.din("ident", (128, 128))
    if addx:
        xin = p.din("xin", (SH, D))
    yo = p.dout("ynew", (SH, D))

    id_f = p.sb("id_f", (128, 128))
    id_bf = p.sb("id_bf", (128, 128), BF16)
    w1bf = p.sb("w1bf", (128, 8 * W1C), BF16)
    w1tmp = p.sb("w1tmp", (128, W1C))
    onesc = p.sb("onesc", (128, 1), BF16)
    ones1 = p.sb("ones1", (1, 128), BF16)
    lcol = p.sb("lcol", (128, 16))
    b1col = p.sb("b1col", (128, 44))
    b2row = p.sb("b2row", (1, D))
    b2bf = p.sb("b2bf", (1, D), BF16)
    yp_ch = p.sb("yp_ch", (128, 4 * D))
    ybf = p.sb("ybf", (128, D), BF16)
    yfm = p.sb("yfm", (128, 8 * 513), BF16)
    ycol8 = p.sb("ycol8", (128, 8))
    hn_fm = p.sb("hn_fm", (128, 8 * 512), BF16)
    sqbf = p.sb("sqbf", (128, 512), BF16)
    rows = p.sb("rows", (1, 8 * 512))
    rows_bf = p.sb("rows_bf", (1, 2 * 512), BF16)
    gel_f = p.sb("gel_f", (128, 512))
    t1 = p.sb("t1", (128, 512))
    hmidbf = p.sb("hmidbf", (128, KD * 512), BF16)
    w2tmp = p.sb("w2tmp", (128, 512))
    w2bf = p.sb("w2bf", (128, 512), BF16)
    x_sb = p.sb("x_sb", (128, D))
    out_sb = p.sb("out_sb", (128, 512))
    TP = p.ps("TP", (128, 128), BF16)
    PR = p.ps("PR", (1, 512))
    PS1 = p.ps("PS1", (128, 512))
    PS2 = p.ps("PS2", (128, 512))
    DP = [p.ps(f"DP{i}", (128, 512)) for i in range(4)]

    p.dma(id_f[:, :], ident[:, :])
    p.v('tensor_copy', id_bf[:, :], id_f[:, :])
    p.v('memset', onesc[:, :], 1.0)
    p.v('memset', ones1[:, :], 1.0)
    p.dma(lcol[:, 0:8], lg.rearrange("(k p) -> p k", p=128), allow_slow_non_contiguous=True)
    p.dma(lcol[:, 8:16], lb.rearrange("(k p) -> p k", p=128), allow_slow_non_contiguous=True)
    p.dma(b1col[:, :], b1.rearrange("(k p) -> p k", p=128), allow_slow_non_contiguous=True)
    p.dma(b2row[0:1, :], b2[None, :])
    p.v('tensor_copy', b2bf[:, :], b2row[:, :])
    for k in range(8):
        p.dma(w1tmp[:, :], w1[k * 128:(k + 1) * 128, :])
        p.v('tensor_copy', w1bf[:, k * W1C:(k + 1) * W1C], w1tmp[:, :])

    for ch in range(4):
        p.dma(ycol8[:, :], yp[ch * 512:ch * 512 + 1, :].rearrange("o (k pp) -> (o pp) k", pp=128),
              allow_slow_non_contiguous=True)
        for k in range(8):
            p.v('tensor_copy', yfm[:, k * 513:k * 513 + 1], ycol8[:, k:k + 1])
        for tl in range(4):
            r = slice(1 + ch * 512 + tl * 128, 1 + ch * 512 + (tl + 1) * 128)
            p.dma(yp_ch[:, tl * D:(tl + 1) * D], yp[r, :])
            p.v('tensor_copy', ybf[:, :], yp_ch[:, tl * D:(tl + 1) * D])
            for k in range(8):
                p.trans(TP[:, :], ybf[:, k * 128:(k + 1) * 128], id_bf[:, :])
                p.v('tensor_copy', yfm[:, k * 513 + 1 + tl * 128:k * 513 + 1 + (tl + 1) * 128], TP[:, :])
        hsl = [slice(k * 513 + (1 if k < 4 else 0), k * 513 + (513 if k < 4 else 512)) for k in range(8)]
        for k in range(8):
            p.mm(PR[0:1, :], onesc[:, :], yfm[:, hsl[k]], start=(k == 0), stop=(k == 7))
        for k in range(8):
            p.act(sqbf[:, :], yfm[:, hsl[k]], AF.Square)
            p.mm(DP[0][0:1, 0:512], onesc[:, :], sqbf[:, :], start=(k == 0), stop=(k == 7))
        p.v('tensor_scalar_mul', rows[0:1, 0:512], PR[0:1, :], -1.0 / D)
        p.act(rows[0:1, 512:1024], rows[0:1, 0:512], AF.Square)
        p.v('tensor_scalar_mul', rows[0:1, 1024:1536], DP[0][0:1, 0:512], 1.0 / D)
        p.v('tensor_sub', rows[0:1, 1536:2048], rows[0:1, 1024:1536], rows[0:1, 512:1024])
        p.v('tensor_scalar', rows[0:1, 1536:2048], rows[0:1, 1536:2048], 1.0, 1e-5, ALU.mult, ALU.add)
        p.act(rows[0:1, 1536:2048], rows[0:1, 1536:2048], AF.Ln)
        p.act(rows[0:1, 1536:2048], rows[0:1, 1536:2048], AF.Exp, scale=-0.5)
        p.v('tensor_copy', rows_bf[0:1, 0:512], rows[0:1, 0:512])
        p.v('tensor_copy', rows_bf[0:1, 512:1024], rows[0:1, 1536:2048])
        p.mm(PS1[:, :], ones1[:, :], rows_bf[0:1, 0:512])
        p.mm(PS2[:, :], ones1[:, :], rows_bf[0:1, 512:1024])
        for k in range(8):
            p.v('tensor_add', t1[:, :], yfm[:, hsl[k]], PS1[:, :])
            p.v('tensor_mul', t1[:, :], t1[:, :], PS2[:, :])
            p.v('tensor_scalar', hn_fm[:, k * 512:(k + 1) * 512], t1[:, :],
                lcol[:, k:k + 1], lcol[:, 8 + k:9 + k], ALU.mult, ALU.add)
        for i in range(KD):
            for k in range(8):
                p.mm(PS1[:, :], w1bf[:, k * W1C + (KD + i) * 128:k * W1C + (KD + i) * 128 + 128],
                     hn_fm[:, k * 512:(k + 1) * 512], start=(k == 0), stop=(k == 7))
            p.act(gel_f[:, :], PS1[:, :], AF.Gelu, bias=b1col[:, KD + i:KD + i + 1])
            for k in range(8):
                p.mm(PS2[:, :], w1bf[:, k * W1C + i * 128:k * W1C + i * 128 + 128],
                     hn_fm[:, k * 512:(k + 1) * 512], start=(k == 0), stop=(k == 7))
            p.v('tensor_scalar_add', t1[:, :], PS2[:, :], b1col[:, i:i + 1])
            p.v('tensor_mul', hmidbf[:, i * 512:(i + 1) * 512], t1[:, :], gel_f[:, :])
        for n in range(2):
            for k in range(KD):
                p.dma(w2tmp[:, :], w2[k * 128:(k + 1) * 128, n * 512:(n + 1) * 512])
                p.v('tensor_copy', w2bf[:, :], w2tmp[:, :])
                for mt in range(4):
                    p.mm(DP[mt][:, :], hmidbf[:, k * 512 + mt * 128:k * 512 + mt * 128 + 128],
                         w2bf[:, :], start=(k == 0), stop=False)
            for mt in range(4):
                p.mm(DP[mt][:, :], ones1[:, :], b2bf[:, n * 512:(n + 1) * 512],
                     start=False, stop=True)
            for mt in range(4):
                t = ch * 4 + mt
                p.v('tensor_add', out_sb[:, :], yp_ch[:, mt * D + n * 512:mt * D + n * 512 + 512],
                    DP[mt][:, :])
                if addx:
                    p.dma(x_sb[:, 0:512], xin[t * 128:(t + 1) * 128, n * 512:(n + 1) * 512])
                    p.v('tensor_add', out_sb[:, :], out_sb[:, :], x_sb[:, 0:512])
                p.dma(yo[t * 128:(t + 1) * 128, n * 512:(n + 1) * 512], out_sb[:, :])
    return p


def _l0ffn_tables():
    hd = 64
    half = np.arange(0, hd, 2, dtype=np.float32)
    inv = 1.0 / 10000.0 ** (half / hd)
    t = np.arange(SEQ, dtype=np.float32)
    f = t[:, None] * inv[None, :]
    base = (half + 0.4 * hd) / (1.4 * hd)
    power = (t - SEQ // 2) / 256.0
    scale = base[None, :] ** power[:, None]
    cos, sin = np.cos(f), np.sin(f)
    sgn = np.concatenate([-np.ones((SEQ, 32), np.float32), np.ones((SEQ, 32), np.float32)], 1)
    c2 = np.concatenate([cos, cos], 1); s2 = np.concatenate([sin, sin], 1)
    sc2 = np.concatenate([scale, scale], 1)
    cq = c2 * sc2 * 8.0
    sq_ = s2 * sc2 * 8.0 * sgn
    ck = c2 / sc2
    sk = s2 / sc2 * sgn
    tile = lambda a: np.tile(np.tile(a, (1, 8)), (2, 1))
    return tile(cq), tile(sq_), tile(ck), tile(sk)


def _l0ffn_masks():
    qi = np.arange(512)
    mreg = np.zeros((1024, 512), np.float32)
    for kb in range(8):
        for rk in range(128):
            j = kb * 128 + rk
            mreg[kb * 128 + rk] = np.where((qi >= j - 512) & (qi <= j), 0.0, NEG)
    mw0 = mreg.copy()
    mw0[0:512, :] = NEG
    return mreg, mw0


def block_l0ffn(x, ins):
    IN = 512
    cq, sq_, ck, sk = _l0ffn_tables()
    mreg, mw0 = _l0ffn_masks()
    ident = np.eye(128, dtype=np.float32)
    HID = 2730
    y = run_PLN(x)
    for l in range(2):
        wqkv = f32(ins['L0_ffn_wqkv'][l])
        qsc = f32(ins['L0_ffn_q_scale'][l]); ksc = f32(ins['L0_ffn_k_scale'][l])
        wo = f32(ins['L0_ffn_wo'][l])
        lg = f32(ins['L0_ffn_ln_g'][l]); lb = f32(ins['L0_ffn_ln_b'][l])
        w1 = f32(ins['L0_ffn_w1'][l]); b1 = f32(ins['L0_ffn_b1'][l])
        w2 = f32(ins['L0_ffn_w2'][l]); b2 = f32(ins['L0_ffn_b2'][l])
        w1re = np.zeros((D, 5632), np.float32)
        w1re[:, 0:HID] = w1[:, 0:HID]
        w1re[:, 2816:2816 + HID] = w1[:, HID:2 * HID]
        b1re = np.zeros(5632, np.float32)
        b1re[0:HID] = b1[0:HID]; b1re[2816:2816 + HID] = b1[HID:2 * HID]
        w2p = np.zeros((2816, D), np.float32)
        w2p[0:HID] = w2
        im = []
        for c in range(NCORE):
            s = slice(c * SH, (c + 1) * SH)
            ym1 = np.zeros((1, D), np.float32) if c % 4 == 0 else f32(y[c * SH - 1:c * SH])
            im.append({"y": f32(y[s]), "ym1": ym1, "wqkv": wqkv, "qsc": qsc, "ksc": ksc,
                       "tabq": np.ascontiguousarray(np.concatenate([cq[s], sq_[s]], 1)),
                       "tabk": np.ascontiguousarray(np.concatenate([ck[s], sk[s]], 1)),
                       "ident": ident})
        res = runp(build_P2a, ("P2a",), im)
        qfm = np.concatenate([r["qfm"] for r in res], axis=1)
        kfm = np.concatenate([r["kfm"] for r in res], axis=1)
        v = np.concatenate([r["v"] for r in res], axis=0)
        zc = np.zeros((IN, 512), ml_dtypes.bfloat16)
        zv = np.zeros((512, IN), ml_dtypes.bfloat16)
        im = []
        for c in range(NCORE):
            s = slice(c * SH, (c + 1) * SH)
            if c % 4 == 0:
                kh = np.concatenate([zc, qfm[:, 0:0] if False else kfm[:, s]], axis=1)
                kh = np.concatenate([zc, kfm[:, s]], axis=1)
                vh = np.concatenate([zv, v[s]], axis=0)
            else:
                kh = np.ascontiguousarray(kfm[:, c * SH - 512:(c + 1) * SH])
                vh = np.ascontiguousarray(v[c * SH - 512:(c + 1) * SH])
            im.append({"qfm": np.ascontiguousarray(qfm[:, s]), "kfm": kh, "v": vh,
                       "mreg": mreg, "mw0": mw0 if c % 4 == 0 else mreg})
        res = runp(build_P2b, ("P2b",), im)
        ofm = np.concatenate([r["ofm"] for r in res], axis=1)
        zofm = np.zeros((IN, 128), ml_dtypes.bfloat16)
        zy = np.zeros((128, D), np.float32)
        im = []
        for c in range(NCORE):
            if c % 4 == 0:
                yh = np.concatenate([zy, y[c * SH:(c + 1) * SH]], axis=0)
                oh_ = np.concatenate([zofm, ofm[:, c * SH:(c + 1) * SH]], axis=1)
            else:
                yh = f32(y[c * SH - 128:(c + 1) * SH])
                oh_ = np.ascontiguousarray(ofm[:, c * SH - 128:(c + 1) * SH])
            im.append({"y": yh, "ofm": oh_, "wo": wo})
        res = runp(build_P2c1, ("P2c1",), im)
        yps = [r["yp"] for r in res]
        addx = (l == 1)
        im = []
        for c in range(NCORE):
            d = {"yp": f32(yps[c][127:]), "w1re": w1re, "b1re": b1re, "w2p": w2p,
                 "b2": b2, "lg": lg, "lb": lb, "ident": ident}
            if addx:
                d["xin"] = f32(x[c * SH:(c + 1) * SH])
            im.append(d)
        res = runp(lambda: build_P2c2(addx), ("P2c2", addx), im)
        y = np.concatenate([r["ynew"] for r in res], axis=0)
    return y



def build_P1a():
    HT = 2176
    p = Prog("P1a")
    x = p.din("x", (SH, D))
    xpre = p.din("xpre", (128, D))
    qwT = p.din("qwT", (D, D))
    kwT = p.din("kwT", (D, D))
    vwT = p.din("vwT", (D, D))
    bwT = p.din("bwT", (D, 4))
    convq = p.din("convq", (D, 4))
    convk = p.din("convk", (D, 4))
    convv = p.din("convv", (D, 4))
    firs = p.din("firs", (D, 3))
    firl = p.din("firl", (D, 31))
    firw = p.din("firw", (D, 64))
    ident = p.din("ident", (128, 128))
    o_qn = p.dout("qn_fm", (D, SH))
    o_kn = p.dout("kn_fm", (D, SH))
    o_kbn = p.dout("kbn_fm", (D, SH))
    o_kt = p.dout("kn_tm", (SH, D))
    o_kbt = p.dout("kbn_tm", (SH, D))
    o_vbt = p.dout("vb_tm", (SH, D))
    o_vt = p.dout("v_tm", (SH, D))
    o_st = p.dout("s_tm", (SH, D))
    o_lt = p.dout("l_tm", (SH, D))
    o_wt = p.dout("w_tm", (SH, D))
    o_stats = p.dout("stats48", (SH, 48))

    id_f = p.sb("id_f", (128, 128))
    id_bf = p.sb("id_bf", (128, 128), BF16)
    onesc = p.sb("onesc", (128, 1), BF16)
    ones1 = p.sb("ones1", (1, 128), BF16)
    wld = p.sb("wld", (128, D))
    wldbf = p.sb("wldbf", (128, D), BF16)
    wsb = {"qwT": qwT, "kwT": kwT, "vwT": vwT}
    wtmp = p.sb("wtmp", (128, D))
    bw_sb = p.sb("bw_sb", (128, 8 * 4), BF16)
    cv_sb = p.sb("cv_sb", (128, 8 * (4 * 3 + 3 + 31 + 64)))
    CW = 4 * 3 + 3 + 31 + 64
    x_sb = p.sb("x_sb", (128, D))
    xn = p.sb("xn", (128, D))
    sq = p.sb("sq", (128, D))
    col = p.sb("col", (128, 2))
    xbf = p.sb("xbf", (128, D), BF16)
    xfm = p.sb("xfm", (128, 8 * HT), BF16)
    beta = p.sb("beta", (128, 16 * 4))
    psil = p.sb("psil", (128, 2 * HT))
    pacc = p.sb("pacc", (128, HT))
    pacc2 = p.sb("pacc2", (128, HT))
    rows = p.sb("rows", (1, HT))
    onesc_f = p.sb("onesc_f", (128, 1))
    ones1_f = p.sb("ones1_f", (1, 128))
    qn_sb = p.sb("qn_sb", (128, 2 * HT))
    vv = p.sb("vv", (128, 2 * HT))
    fir3 = p.sb("fir3", (128, 3 * 2 * HT))
    tmt = p.sb("tmt", (128, 256))
    tmt2 = p.sb("tmt2", (128, 256))
    st_sb = p.sb("st_sb", (128, 16 * 48))
    TPb = p.ps("TPb", (128, 128), BF16)
    TPf = p.ps("TPf", (128, 128))
    PRJ = p.ps("PRJ", (128, 512))
    RW = p.ps("RW", (1, 512))
    BCT = p.ps("BCT", (128, 512))
    BETp = p.ps("BETp", (128, 4))

    NT5 = [(0, 512), (512, 512), (1024, 512), (1536, 512), (2048, 128)]

    p.v('memset', onesc[:, :], 1.0)
    p.v('memset', ones1[:, :], 1.0)
    p.v('memset', onesc_f[:, :], 1.0)
    p.v('memset', ones1_f[:, :], 1.0)
    p.dma(id_f[:, :], ident[:, :])
    p.v('tensor_copy', id_bf[:, :], id_f[:, :])
    for k in range(8):
        p.dma(wtmp[:, 0:4], bwT[k * 128:(k + 1) * 128, :])
        p.v('tensor_copy', bw_sb[:, k * 4:(k + 1) * 4], wtmp[:, 0:4])
        base = k * CW
        p.dma(cv_sb[:, base:base + 4], convq[k * 128:(k + 1) * 128, :])
        p.dma(cv_sb[:, base + 4:base + 8], convk[k * 128:(k + 1) * 128, :])
        p.dma(cv_sb[:, base + 8:base + 12], convv[k * 128:(k + 1) * 128, :])
        p.dma(cv_sb[:, base + 12:base + 15], firs[k * 128:(k + 1) * 128, :])
        p.dma(cv_sb[:, base + 15:base + 46], firl[k * 128:(k + 1) * 128, :])
        p.dma(cv_sb[:, base + 46:base + 110], firw[k * 128:(k + 1) * 128, :])

    for t in range(17):
        if t == 0:
            p.dma(x_sb[:, :], xpre[:, :])
        else:
            p.dma(x_sb[:, :], x[(t - 1) * 128:t * 128, :])
        p.ln_tok(xn[:, :], x_sb[:, :], sq[:, :], col)
        p.v('tensor_copy', xbf[:, :], xn[:, :])
        for k in range(8):
            p.trans(TPb[:, :], xbf[:, k * 128:(k + 1) * 128], id_bf[:, :])
            p.v('tensor_copy', xfm[:, k * HT + t * 128:k * HT + (t + 1) * 128], TPb[:, :])
    for t in range(16):
        tc = (t + 1) * 128
        for k in range(8):
            p.mm(BETp[:, :], xfm[:, k * HT + tc:k * HT + tc + 128], bw_sb[:, k * 4:(k + 1) * 4],
                 start=(k == 0), stop=(k == 7))
        p.act(beta[:, t * 4:(t + 1) * 4], BETp[:, :], AF.Sigmoid)

    def proj_conv(widx, ch, coff, K, dst, dcol):
        W = wsb[widx]
        for k in range(8):
            p.dma(wld[:, k * 128:(k + 1) * 128], W[k * 128:(k + 1) * 128, ch * 128:(ch + 1) * 128])
        p.v('tensor_copy', wldbf[:, :], wld[:, :])
        for (o_, n_) in NT5:
            for k in range(8):
                p.mm(PRJ[:, 0:n_], wldbf[:, k * 128:(k + 1) * 128],
                     xfm[:, k * HT + o_:k * HT + o_ + n_], start=(k == 0), stop=(k == 7))
            p.v('tensor_copy', pacc[:, o_:o_ + n_], PRJ[:, 0:n_])
        base = ch * CW + coff
        p.v('memset', pacc2[:, 0:K - 1], 0.0)
        p.v('tensor_scalar_mul', pacc2[:, K - 1:HT], pacc[:, 0:HT - K + 1], cv_sb[:, base:base + 1])
        for j in range(1, K):
            p.v('scalar_tensor_tensor', pacc2[:, K - 1:HT], pacc[:, j:HT - K + 1 + j],
                cv_sb[:, base + j:base + j + 1], pacc2[:, K - 1:HT], ALU.mult, ALU.add)
        p.act(dst[:, dcol:dcol + HT], pacc2[:, 0:HT], AF.Silu)

    outs = (o_qn, o_kn, o_kbn, o_kt, o_kbt, o_vbt, o_vt, o_st, o_lt, o_wt, o_stats)
    C = (HT, xfm, wsb, cv_sb, beta, psil, pacc, pacc2, onesc_f, rows, ones1_f, qn_sb, vv, fir3,
         tmt, tmt2, st_sb, TPb, TPf, PRJ, RW, BCT, onesc, ones1, id_f, id_bf, proj_conv,
         outs)
    _p1a_heads(p, C)
    return p

def _p1a_heads(p, C):
    (HT, xfm, wsb, cv_sb, beta, psil, pacc, pacc2, onesc_f, rows, ones1_f, qn_sb, vv, fir3,
     tmt, tmt2, st_sb, TPb, TPf, PRJ, RW, BCT, onesc, ones1, id_f, id_bf, proj_conv,
     outs) = C
    (o_qn, o_kn, o_kbn, o_kt, o_kbt, o_vbt, o_vt, o_st, o_lt, o_wt, o_stats) = outs
    CW = 110
    NT5 = [(0, 512), (512, 512), (1024, 512), (1536, 512), (2048, 128)]

    def l2norm_pair(dst):
        for (o_, n_) in NT5:
            for cc in range(2):
                p.act(pacc[:, o_:o_ + n_], psil[:, cc * HT + o_:cc * HT + o_ + n_], AF.Square)
                p.mm(RW[0:1, 0:n_], onesc_f[:, :], pacc[:, o_:o_ + n_],
                     start=(cc == 0), stop=(cc == 1))
            p.v('tensor_copy', rows[0:1, o_:o_ + n_], RW[0:1, 0:n_])
        p.v('tensor_scalar', rows[0:1, :], rows[0:1, :], 1.0, 1e-6, ALU.mult, ALU.add)
        p.act(rows[0:1, :], rows[0:1, :], AF.Ln)
        p.act(rows[0:1, :], rows[0:1, :], AF.Exp, scale=-0.5)
        for (o_, n_) in NT5:
            p.mm(BCT[:, 0:n_], ones1_f[:, :], rows[0:1, o_:o_ + n_])
            for cc in range(2):
                p.v('tensor_mul', dst[:, cc * HT + o_:cc * HT + o_ + n_],
                    psil[:, cc * HT + o_:cc * HT + o_ + n_], BCT[:, 0:n_])

    for h in range(4):
        for cc in range(2):
            proj_conv("qwT", 2 * h + cc, 0, 4, psil, cc * HT)
        l2norm_pair(qn_sb)
        for cc in range(2):
            p.dma(o_qn[(2 * h + cc) * 128:(2 * h + cc + 1) * 128, :], qn_sb[:, cc * HT + 128:cc * HT + HT])
        for cc in range(2):
            proj_conv("kwT", 2 * h + cc, 4, 4, psil, cc * HT)
        l2norm_pair(qn_sb)
        for cc in range(2):
            p.dma(o_kn[(2 * h + cc) * 128:(2 * h + cc + 1) * 128, :], qn_sb[:, cc * HT + 128:cc * HT + HT])
        for t in range(16):
            tc = 128 + t * 128
            for cc in range(2):
                p.trans(TPf[:, :], qn_sb[:, cc * HT + tc:cc * HT + tc + 128], id_f[:, :])
                p.v('tensor_copy', tmt[:, cc * 128:(cc + 1) * 128], TPf[:, :])
            p.dma(o_kt[t * 128:(t + 1) * 128, h * 256:(h + 1) * 256], tmt[:, 0:256])
            p.v('tensor_scalar_mul', tmt2[:, 0:256], tmt[:, 0:256], beta[:, t * 4 + h:t * 4 + h + 1])
            p.dma(o_kbt[t * 128:(t + 1) * 128, h * 256:(h + 1) * 256], tmt2[:, 0:256])
            p.v('tensor_copy', tmt[:, 0:256], tmt2[:, 0:256])
            for cc in range(2):
                p.trans(TPf[:, :], tmt[:, cc * 128:(cc + 1) * 128], id_f[:, :])
                p.v('tensor_copy', tmt2[:, cc * 128:(cc + 1) * 128], TPf[:, :])
                p.dma(o_kbn[(2 * h + cc) * 128:(2 * h + cc + 1) * 128, t * 128:(t + 1) * 128],
                      tmt2[:, cc * 128:(cc + 1) * 128])
        for cc in range(2):
            proj_conv("vwT", 2 * h + cc, 8, 4, vv, cc * HT)
            ch = 2 * h + cc
            for (fi, (coff, K)) in enumerate(((12, 3), (15, 31), (46, 64))):
                base = ch * CW + coff
                dst = fir3[:, fi * 2 * HT + cc * HT:fi * 2 * HT + (cc + 1) * HT]
                p.v('memset', dst[:, 0:K - 1], 0.0)
                p.v('tensor_scalar_mul', dst[:, K - 1:HT], vv[:, cc * HT:cc * HT + HT - K + 1],
                    cv_sb[:, base:base + 1])
                for j in range(1, K):
                    p.v('scalar_tensor_tensor', dst[:, K - 1:HT], vv[:, cc * HT + j:cc * HT + HT - K + 1 + j],
                        cv_sb[:, base + j:base + j + 1], dst[:, K - 1:HT], ALU.mult, ALU.add)
        for (ti, (srcbuf, o_tm, grp)) in enumerate((
                (vv, o_vt, 3), (fir3, o_st, 0), (None, o_lt, 1), (None, o_wt, 2))):
            for t in range(16):
                tc = 128 + t * 128
                for cc in range(2):
                    if ti == 0:
                        s_ = vv[:, cc * HT + tc:cc * HT + tc + 128]
                    else:
                        fi = ti - 1
                        s_ = fir3[:, fi * 2 * HT + cc * HT + tc:fi * 2 * HT + cc * HT + tc + 128]
                    p.trans(TPf[:, :], s_, id_f[:, :])
                    p.v('tensor_copy', tmt[:, cc * 128:(cc + 1) * 128], TPf[:, :])
                p.dma(o_tm[t * 128:(t + 1) * 128, h * 256:(h + 1) * 256], tmt[:, 0:256])
                if ti == 0:
                    p.v('tensor_scalar_mul', tmt2[:, 0:256], tmt[:, 0:256],
                        beta[:, t * 4 + h:t * 4 + h + 1])
                    p.dma(o_vbt[t * 128:(t + 1) * 128, h * 256:(h + 1) * 256], tmt2[:, 0:256])
                sc = grp * 12
                p.v('reduce_sum', st_sb[:, t * 48 + sc + h:t * 48 + sc + h + 1], tmt[:, 0:256], AX.X)
                p.v('tensor_scalar_mul', st_sb[:, t * 48 + sc + h:t * 48 + sc + h + 1], st_sb[:, t * 48 + sc + h:t * 48 + sc + h + 1], 1.0 / 256)
                p.act(tmt2[:, 0:256], tmt[:, 0:256], AF.Square)
                p.v('reduce_sum', st_sb[:, t * 48 + sc + 4 + h:t * 48 + sc + 5 + h], tmt2[:, 0:256], AX.X)
                p.v('tensor_scalar', st_sb[:, t * 48 + sc + 4 + h:t * 48 + sc + 5 + h], st_sb[:, t * 48 + sc + 4 + h:t * 48 + sc + 5 + h],
                    1.0 / 256, 0.0, ALU.mult, ALU.add)
                p.v('tensor_scalar_max', st_sb[:, t * 48 + sc + 4 + h:t * 48 + sc + 5 + h], st_sb[:, t * 48 + sc + 4 + h:t * 48 + sc + 5 + h], 1e-8)
                p.s('sqrt', st_sb[:, t * 48 + sc + 4 + h:t * 48 + sc + 5 + h], st_sb[:, t * 48 + sc + 4 + h:t * 48 + sc + 5 + h])
                p.v('reduce_max', st_sb[:, t * 48 + sc + 8 + h:t * 48 + sc + 9 + h], tmt[:, 0:256], AX.X,
                    apply_absolute_value=True)
    for t in range(16):
        p.dma(o_stats[t * 128:(t + 1) * 128, :], st_sb[:, t * 48:(t + 1) * 48])
    return p



def build_P1b():
    p = Prog("P1b")
    qfm = p.din("qfm", (256, SEQ))
    kfm = p.din("kfm", (256, SEQ))
    kbfm = p.din("kbfm", (256, SEQ))
    ktm = p.din("ktm", (SEQ, 256))
    kbtm = p.din("kbtm", (SEQ, 256))
    vbtm = p.din("vbtm", (SEQ, 256))
    eye_i = p.din("eye", (32, 32))
    msk_i = p.din("msk", (32, 128))
    dout = p.dout("dout", (SEQ, 256))

    qf = p.sb("qf", (128, 2 * 512))
    kf = p.sb("kf", (128, 2 * 512))
    kbf = p.sb("kbf", (128, 2 * 512))
    kt = p.sb("kt", (32, 16 * 256))
    kbt = p.sb("kbt", (32, 16 * 256))
    vbt = p.sb("vbt", (32, 16 * 256))
    eye = p.sb("eye_sb", (32, 32))
    msk = p.sb("msk_sb", (32, 128))
    sm = p.sb("sm", (32, 32 * 20))
    S_sb = p.sb("S_sb", (128, 512))
    u_sb = p.sb("u_sb", (32, 256))
    uadj = p.sb("uadj", (32, 256))
    wfm = p.sb("wfm", (128, 64))
    o_sb = p.sb("o_sb", (32, 256))
    PA = p.ps("PA", (32, 512))
    PB = p.ps("PB", (32, 512))
    PW = p.ps("PW", (128, 64))
    S_ps = p.ps("S_ps", (128, 512))

    names = ["A", "AT", "A2", "A2T", "A4", "A4T", "A8", "A8T", "T1", "T1T", "T2", "T2T",
             "T4", "T4T", "T8", "T8T", "T16", "T16T", "P1", "P1T"]
    slot = {n: sm[:, i * 32:(i + 1) * 32] for i, n in enumerate(names)}
    slot["P2"] = slot["A2"]; slot["P2T"] = slot["A2T"]
    slot["Q"] = slot["A4"]; slot["QT"] = slot["A4T"]
    slot["inv"] = slot["A8"]; slot["invT"] = slot["A8T"]

    p.dma(eye[:, :], eye_i[:, :])
    p.dma(msk[:, :], msk_i[:, :])
    p.v('memset', S_sb[:, :], 0.0)
    trils = msk[:, 0:32]; trilsT = msk[:, 32:64]; tril = msk[:, 64:96]; trilu = msk[:, 96:128]

    for seg in range(16):
        ss = slice(seg * 512, (seg + 1) * 512)
        for (buf, src) in ((qf, qfm), (kf, kfm), (kbf, kbfm)):
            for cc in range(2):
                p.dma(buf[:, cc * 512:(cc + 1) * 512], src[cc * 128:(cc + 1) * 128, ss])
        for (buf, src) in ((kt, ktm), (kbt, kbtm), (vbt, vbtm)):
            p.dma(buf.rearrange("p (j c) -> p j c", c=256),
                  src[ss, :].rearrange("(j p) c -> p j c", p=32))
        for j in range(16):
            ci = seg * 16 + j
            jA = j * 32

            def fc(buf, cc):
                return buf[:, cc * 512 + jA:cc * 512 + jA + 32]
            p.mm(PA[:, 0:32], fc(kbf, 0), fc(kf, 0), start=True, stop=False)
            p.mm(PA[:, 0:32], fc(kbf, 1), fc(kf, 1), start=False, stop=True)
            p.mm(PA[:, 32:64], fc(kf, 0), fc(kbf, 0), start=True, stop=False)
            p.mm(PA[:, 32:64], fc(kf, 1), fc(kbf, 1), start=False, stop=True)
            p.v('tensor_mul', slot["A"], trils, PA[:, 0:32])
            p.v('tensor_mul', slot["AT"], trilsT, PA[:, 32:64])
            reg = lambda i: PA[:, 64 + i * 32:96 + i * 32]
            sqs = [("A2", "AT", "A"), ("A2T", "A", "AT"), ("A4", "A2T", "A2"),
                   ("A4T", "A2", "A2T"), ("A8", "A4T", "A4"), ("A8T", "A4", "A4T")]
            for i, (dn, l_, r_) in enumerate(sqs):
                p.mm(reg(i), slot[l_], slot[r_])
                p.v('tensor_copy', slot[dn], reg(i))
                p.v('tensor_add', slot["T" + dn[1:]], eye[:, :], reg(i))
            p.mm(reg(6), slot["A8T"], slot["A8"])
            p.v('tensor_add', slot["T16"], eye[:, :], reg(6))
            p.mm(reg(7), slot["A8"], slot["A8T"])
            p.v('tensor_add', slot["T16T"], eye[:, :], reg(7))
            p.v('tensor_sub', slot["T1"], eye[:, :], slot["A"])
            p.v('tensor_sub', slot["T1T"], eye[:, :], slot["AT"])
            prods = [("P1", "T1T", "T2", 8), ("P1T", "T2", "T1T", 9), ("P2", "T4T", "T8", 10),
                     ("P2T", "T8", "T4T", 11), ("Q", "P1T", "P2", 12), ("QT", "P2", "P1T", 13),
                     ("inv", "QT", "T16", 0), ("invT", "T16", "QT", 1)]
            for (dn, l_, r_, ri) in prods:
                p.mm(reg(ri), slot[l_], slot[r_])
                p.v('tensor_copy', slot[dn], reg(ri))
            p.mm(PB[:, 0:256], slot["invT"], vbt[:, j * 256:(j + 1) * 256])
            p.v('tensor_copy', u_sb[:, :], PB[:, 0:256])
            p.mm(PW[:, 0:32], kbt[:, j * 256:j * 256 + 128], slot["invT"])
            p.mm(PW[:, 32:64], kbt[:, j * 256 + 128:j * 256 + 256], slot["invT"])
            p.v('tensor_copy', wfm[:, :], PW[:, :])
            p.mm(PB[:, 256:512][:, 0:256], wfm[:, 0:32], S_sb[:, 0:256], start=True, stop=False)
            p.mm(PB[:, 256:512][:, 0:256], wfm[:, 32:64], S_sb[:, 256:512], start=False, stop=True)
            p.v('tensor_sub', uadj[:, :], u_sb[:, :], PB[:, 256:512])
            p.mm(PA[:, 64:96], fc(qf, 0), fc(kf, 0), start=True, stop=False)
            p.mm(PA[:, 64:96], fc(qf, 1), fc(kf, 1), start=False, stop=True)
            p.mm(PA[:, 96:128], fc(kf, 0), fc(qf, 0), start=True, stop=False)
            p.mm(PA[:, 96:128], fc(kf, 1), fc(qf, 1), start=False, stop=True)
            p.v('tensor_mul', slot["A2"], tril, PA[:, 64:96])
            p.v('tensor_mul', slot["A2T"], trilu, PA[:, 96:128])
            p.mm(PB[:, 0:256], fc(qf, 0), S_sb[:, 0:256], start=True, stop=False)
            p.mm(PB[:, 0:256], fc(qf, 1), S_sb[:, 256:512], start=False, stop=False)
            p.mm(PB[:, 0:256], slot["A2T"], uadj[:, :], start=False, stop=True)
            p.v('tensor_copy', o_sb[:, :], PB[:, 0:256])
            p.dma(dout[ci * 32:(ci + 1) * 32, :], o_sb[:, :])
            p.mm(S_ps[:, 0:256], kt[:, j * 256:j * 256 + 128], uadj[:, :],
                 start=(ci == 0), stop=True)
            p.mm(S_ps[:, 256:512], kt[:, j * 256 + 128:j * 256 + 256], uadj[:, :],
                 start=(ci == 0), stop=True)
            p.v('tensor_copy', S_sb[:, :], S_ps[:, :])
    return p



def build_P1c():
    GH = 2048
    p = Prog("P1c")
    x = p.din("x", (SH, D))
    stats48 = p.din("stats48", (SH, 48))
    dtm = p.din("d_tm", (SH, D))
    vtm = p.din("v_tm", (SH, D))
    stm = p.din("s_tm", (SH, D))
    ltm = p.din("l_tm", (SH, D))
    wtm = p.din("w_tm", (SH, D))
    g1wT = p.din("g1wT", (1084, GH))
    g1b = p.din("g1b", (GH,))
    g2wT = p.din("g2wT", (GH, 20))
    g2b = p.din("g2b", (20, 1))
    vb20 = p.din("vb20", (20, 1))
    lt20 = p.din("lt20", (20, 1))
    flrep = p.din("floor_rep", (128, 4))
    onrep = p.din("onorm_rep", (128, D))
    owT = p.din("owT", (D, D))
    ident = p.din("ident", (128, 128))
    xo = p.dout("xout", (SH, D))

    id_f = p.sb("id_f", (128, 128))
    id_bf = p.sb("id_bf", (128, 128), BF16)
    g1sb = p.sb("g1sb", (128, 9 * GH), BF16)
    g2sb = p.sb("g2sb", (128, 16 * 20), BF16)
    owsb = p.sb("owsb", (128, 8 * D), BF16)
    wtmp = p.sb("wtmp", (128, GH))
    b1col = p.sb("b1col", (128, 16))
    r20 = p.sb("r20", (20, 3))
    x_sb = p.sb("x_sb", (128, D))
    xln = p.sb("xln", (128, D))
    sq = p.sb("sq", (128, D))
    col = p.sb("col", (128, 2))
    xbf = p.sb("xbf", (128, D), BF16)
    xfm = p.sb("xfm", (128, 8 * 128), BF16)
    st60 = p.sb("st60", (128, 60))
    st60bf = p.sb("st60bf", (128, 60), BF16)
    gin60 = p.sb("gin60", (60, 128), BF16)
    hdn = p.sb("hdn", (128, 16 * 128), BF16)
    lg20 = p.sb("lg20", (20, 128))
    lgt = p.sb("lgt", (128, 20))
    e20 = p.sb("e20", (128, 20))
    fsc = p.sb("fsc", (128, 8))
    w5 = p.sb("w5", (128, 5))
    str_sb = p.sb("str_sb", (128, 5 * D))
    o_acc = p.sb("o_acc", (128, D))
    obf = p.sb("obf", (128, D), BF16)
    out_sb = p.sb("out_sb", (128, D))
    flr = p.sb("flr", (128, 4))
    onw = p.sb("onw", (128, D))
    TPb = p.ps("TPb", (128, 128), BF16)
    TPf = p.ps("TPf", (128, 128))
    HP = p.ps("HP", (128, 128))
    LP = p.ps("LP", (20, 128))
    OP = p.ps("OP", (128, 512))

    p.dma(id_f[:, :], ident[:, :])
    p.v('tensor_copy', id_bf[:, :], id_f[:, :])
    for k in range(8):
        p.dma(wtmp[:, :], g1wT[k * 128:(k + 1) * 128, :])
        p.v('tensor_copy', g1sb[:, k * GH:(k + 1) * GH], wtmp[:, :])
    p.dma(wtmp[0:60, :], g1wT[1024:1084, :])
    p.v('tensor_copy', g1sb[0:60, 8 * GH:9 * GH], wtmp[0:60, :])
    for k in range(16):
        p.dma(wtmp[:, 0:20], g2wT[k * 128:(k + 1) * 128, :])
        p.v('tensor_copy', g2sb[:, k * 20:(k + 1) * 20], wtmp[:, 0:20])
    for k in range(8):
        p.dma(wtmp[:, 0:D], owT[k * 128:(k + 1) * 128, :])
        p.v('tensor_copy', owsb[:, k * D:(k + 1) * D], wtmp[:, 0:D])
    p.dma(b1col[:, :], g1b.rearrange("(k p) -> p k", p=128), allow_slow_non_contiguous=True)
    p.dma(r20[:, 0:1], g2b[:, :])
    p.dma(r20[:, 1:2], vb20[:, :])
    p.dma(r20[:, 2:3], lt20[:, :])
    p.act(r20[:, 2:3], r20[:, 2:3], AF.Exp, scale=-1.0)
    p.dma(flr[:, :], flrep[:, :])
    p.act(flr[:, :], flr[:, :], AF.Sigmoid)
    p.dma(onw[:, :], onrep[:, :])
    p.dma(fsc[:, 0:4], flrep[:, :])
    p.act(fsc[:, 0:4], fsc[:, 0:4], AF.Sigmoid)
    p.v('tensor_scalar', fsc[:, 4:8], fsc[:, 0:4], -4.0, 1.0, ALU.mult, ALU.add)

    for t in range(SH // 128):
        r = slice(t * 128, (t + 1) * 128)
        p.dma(x_sb[:, :], x[r, :])
        p.ln_tok(xln[:, :], x_sb[:, :], sq[:, :], col)
        p.v('tensor_copy', xbf[:, :], xln[:, :])
        for k in range(8):
            p.trans(TPb[:, :], xbf[:, k * 128:(k + 1) * 128], id_bf[:, :])
            p.v('tensor_copy', xfm[:, k * 128:(k + 1) * 128], TPb[:, :])
        p.dma(str_sb[:, 0:D], stm[r, :])
        p.dma(str_sb[:, D:2 * D], ltm[r, :])
        p.dma(str_sb[:, 2 * D:3 * D], wtm[r, :])
        p.dma(str_sb[:, 3 * D:4 * D], dtm[r, :])
        p.dma(str_sb[:, 4 * D:5 * D], vtm[r, :])
        p.dma(st60[:, 0:36], stats48[r, 0:36])
        p.dma(st60[:, 48:60], stats48[r, 36:48])
        for h in range(4):
            dsl = str_sb[:, 3 * D + h * 256:3 * D + (h + 1) * 256]
            p.v('reduce_sum', st60[:, 36 + h:37 + h], dsl, AX.X)
            p.v('tensor_scalar_mul', st60[:, 36 + h:37 + h], st60[:, 36 + h:37 + h], 1.0 / 256)
            p.act(sq[:, 0:256], dsl, AF.Square)
            p.v('reduce_sum', st60[:, 40 + h:41 + h], sq[:, 0:256], AX.X)
            p.v('tensor_scalar_mul', st60[:, 40 + h:41 + h], st60[:, 40 + h:41 + h], 1.0 / 256)
            p.v('tensor_scalar_max', st60[:, 40 + h:41 + h], st60[:, 40 + h:41 + h], 1e-8)
            p.s('sqrt', st60[:, 40 + h:41 + h], st60[:, 40 + h:41 + h])
            p.v('reduce_max', st60[:, 44 + h:45 + h], dsl, AX.X, apply_absolute_value=True)
        p.v('tensor_copy', st60bf[:, :], st60[:, :])
        p.trans(TPb[0:60, :], st60bf[:, :], id_bf[:, :])
        p.v('tensor_copy', gin60[:, :], TPb[0:60, :])
        for m in range(16):
            for k in range(8):
                p.mm(HP[:, :], g1sb[:, k * GH + m * 128:k * GH + m * 128 + 128],
                     xfm[:, k * 128:(k + 1) * 128], start=(k == 0), stop=False)
            p.mm(HP[:, :], g1sb[0:60, 8 * GH + m * 128:8 * GH + m * 128 + 128],
                 gin60[:, :], start=False, stop=True)
            p.act(hdn[:, m * 128:(m + 1) * 128], HP[:, :], AF.Gelu, bias=b1col[:, m:m + 1])
        for k in range(16):
            p.mm(LP[:, :], g2sb[:, k * 20:(k + 1) * 20], hdn[:, k * 128:(k + 1) * 128],
                 start=(k == 0), stop=(k == 15))
        p.v('tensor_scalar_add', lg20[:, :], LP[:, :], r20[:, 0:1])
        p.v('tensor_scalar_add', lg20[:, :], lg20[:, :], r20[:, 1:2])
        p.v('tensor_scalar_mul', lg20[:, :], lg20[:, :], r20[:, 2:3])
        p.trans(TPf[0:128, 0:20], lg20[:, :], id_f[0:20, 0:20])
        p.v('tensor_copy', lgt[:, :], TPf[0:128, 0:20])
        p.act(e20[:, :], lgt[:, :], AF.Exp)
        p.v('memset', o_acc[:, :], 0.0)
        for h in range(4):
            p.v('reduce_sum', col[:, 0:1], e20[:, 5 * h:5 * h + 5], AX.X)
            p.v('reciprocal', col[:, 1:2], col[:, 0:1])
            p.v('tensor_scalar_mul', w5[:, :], e20[:, 5 * h:5 * h + 5], col[:, 1:2])
            p.v('tensor_scalar_mul', w5[:, :], w5[:, :], fsc[:, 4 + h:5 + h])
            p.v('tensor_scalar_add', w5[:, 0:4], w5[:, 0:4], fsc[:, h:h + 1])
            hs = slice(h * 256, (h + 1) * 256)
            p.v('tensor_scalar_mul', o_acc[:, hs], str_sb[:, h * 256:h * 256 + 256], w5[:, 0:1])
            for (jj, sb_off) in ((1, D), (2, 2 * D), (3, 3 * D), (4, 4 * D)):
                p.v('scalar_tensor_tensor', o_acc[:, hs], str_sb[:, sb_off + h * 256:sb_off + (h + 1) * 256],
                    w5[:, jj:jj + 1], o_acc[:, hs], ALU.mult, ALU.add)
            p.act(sq[:, 0:256], o_acc[:, hs], AF.Square)
            p.v('reduce_sum', col[:, 0:1], sq[:, 0:256], AX.X)
            p.v('tensor_scalar', col[:, 0:1], col[:, 0:1], 1.0 / 256, 1e-5, ALU.mult, ALU.add)
            p.act(col[:, 0:1], col[:, 0:1], AF.Ln)
            p.act(col[:, 0:1], col[:, 0:1], AF.Exp, scale=-0.5)
            p.v('tensor_scalar_mul', o_acc[:, hs], o_acc[:, hs], col[:, 0:1])
        p.v('tensor_mul', o_acc[:, :], o_acc[:, :], onw[:, :])
        p.v('tensor_copy', obf[:, :], o_acc[:, :])
        for k in range(8):
            p.trans(TPb[:, :], obf[:, k * 128:(k + 1) * 128], id_bf[:, :])
            p.v('tensor_copy', xfm[:, k * 128:(k + 1) * 128], TPb[:, :])
        for n in range(2):
            ns = slice(n * 512, (n + 1) * 512)
            for k in range(8):
                p.mm(OP[:, :], xfm[:, k * 128:(k + 1) * 128],
                     owsb[:, k * D + n * 512:k * D + n * 512 + 512], start=(k == 0), stop=(k == 7))
            p.v('tensor_add', out_sb[:, ns], x_sb[:, ns], OP[:, :])
            p.dma(xo[r, ns], out_sb[:, ns])
    return p


def block_deltanet(x, ins):
    ident = np.eye(128, dtype=np.float32)
    f = lambda k: ins[k].astype(np.float32)
    im = []
    for c in range(NCORE):
        s = slice(c * SH, (c + 1) * SH)
        xpre = np.zeros((128, D), np.float32) if c % 4 == 0 else f32(x[c * SH - 128:c * SH])
        im.append({
            "x": f32(x[s]), "xpre": xpre,
            "qwT": f32(f('L0_attn_a_q_w').T), "kwT": f32(f('L0_attn_a_k_w').T),
            "vwT": f32(f('L0_attn_a_v_w').T), "bwT": f32(f('L0_attn_a_b_w').T),
            "convq": f('L0_attn_a_convq_w'), "convk": f('L0_attn_a_convk_w'),
            "convv": f('L0_attn_a_convv_w'),
            "firs": f('L0_attn_a_fir_s').reshape(D, 3), "firl": f('L0_attn_a_fir_l').reshape(D, 31),
            "firw": f('L0_attn_a_fir_wd').reshape(D, 64), "ident": ident,
        })
    res = runp(build_P1a, ("P1a",), im)
    cat0 = lambda nm: np.concatenate([r[nm] for r in res], axis=0)
    cat1 = lambda nm: np.concatenate([r[nm] for r in res], axis=1)
    qn_fm, kn_fm, kbn_fm = cat1("qn_fm"), cat1("kn_fm"), cat1("kbn_fm")
    kn_tm, kbn_tm, vb_tm = cat0("kn_tm"), cat0("kbn_tm"), cat0("vb_tm")
    v_tm, s_tm, l_tm, w_tm = cat0("v_tm"), cat0("s_tm"), cat0("l_tm"), cat0("w_tm")
    stats48 = cat0("stats48")
    eye32 = np.eye(32, dtype=np.float32)
    tl_s = np.tril(np.ones((32, 32), np.float32), -1)
    tl = np.tril(np.ones((32, 32), np.float32), 0)
    msk = np.concatenate([tl_s, tl_s.T, tl, tl.T], axis=1)
    im = []
    for c in range(NCORE):
        b_, h_ = c // 4, c % 4
        rs = slice(h_ * 256, (h_ + 1) * 256)
        cs = slice(b_ * SEQ, (b_ + 1) * SEQ)
        im.append({
            "qfm": np.ascontiguousarray(qn_fm[rs, cs]), "kfm": np.ascontiguousarray(kn_fm[rs, cs]),
            "kbfm": np.ascontiguousarray(kbn_fm[rs, cs]),
            "ktm": np.ascontiguousarray(kn_tm[cs, rs]), "kbtm": np.ascontiguousarray(kbn_tm[cs, rs]),
            "vbtm": np.ascontiguousarray(vb_tm[cs, rs]),
            "eye": eye32, "msk": msk,
        })
    res = runp(build_P1b, ("P1b",), im)
    d_tm = np.zeros((BT, D), np.float32)
    for c in range(NCORE):
        b_, h_ = c // 4, c % 4
        d_tm[b_ * SEQ:(b_ + 1) * SEQ, h_ * 256:(h_ + 1) * 256] = res[c]["dout"]
    g2b = f('L0_attn_a_g2_b')
    vb20 = np.zeros(20, np.float32)
    vb20[4::5] = f('L0_attn_a_value_bias')
    lt20 = np.repeat(f('L0_attn_a_log_temp'), 5)
    im = []
    for c in range(NCORE):
        s = slice(c * SH, (c + 1) * SH)
        im.append({
            "x": f32(x[s]), "stats48": f32(stats48[s]), "d_tm": f32(d_tm[s]),
            "v_tm": f32(v_tm[s]), "s_tm": f32(s_tm[s]), "l_tm": f32(l_tm[s]), "w_tm": f32(w_tm[s]),
            "g1wT": f32(f('L0_attn_a_g1_w').T), "g1b": f('L0_attn_a_g1_b'),
            "g2wT": f32(f('L0_attn_a_g2_w').T), "g2b": g2b.reshape(20, 1), "vb20": vb20.reshape(20, 1), "lt20": f32(lt20).reshape(20, 1),
            "floor_rep": np.tile(f('L0_attn_a_floor_logit'), (128, 1)),
            "onorm_rep": np.tile(f('L0_attn_a_onorm_w'), (128, 4)),
            "owT": f32(f('L0_attn_a_o_w').T), "ident": ident,
        })
    res = runp(build_P1c, ("P1c",), im)
    return np.concatenate([r["xout"] for r in res], axis=0)



def build_P7b():
    M = 256
    DN = 0.0625
    EPS = 0.001
    C4 = 0.3535533905932738
    p = Prog("P7b")
    qfm = p.din("qfm", (256, SEQ), BF16)
    kfm = p.din("kfm", (256, SEQ), BF16)
    vtm = p.din("vtm", (SEQ, 256), BF16)
    ssk = p.din("ssk", (SEQ, 4))
    minss8 = p.din("minss8", (8, 1))
    projT = p.din("projT", (64, M))
    ident = p.din("ident", (128, 128))
    ofm = p.dout("ofm", (256, SEQ), BF16)

    id_f = p.sb("id_f", (128, 128))
    id_bf = p.sb("id_bf", (128, 128), BF16)
    pj_sb = p.sb("pj_sb", (64, M))
    pj_bf = p.sb("pj_bf", (64, M), BF16)
    q_sb = p.sb("q_sb", (64, SEQ), BF16)
    k_sb = p.sb("k_sb", (64, SEQ), BF16)
    v_sb = p.sb("v_sb", (128, 64 * 64), BF16)
    ss_sb = p.sb("ss_c", (128, 64))
    kf_tm = p.sb("kf_tm", (128, 64 * 256), BF16)
    qf_fm = p.sb("qf_fm", (128, 2 * SEQ), BF16)
    onesc = p.sb("onesc", (128, 1), BF16)
    mrow = p.sb("mrow", (1, 8))
    ones_f = p.sb("ones_f", (1, 128))
    stabc = p.sb("stabc", (128, 1))
    bias_c = p.sb("bias_c", (128, 1))
    kcs_bf = p.sb("kcs_bf", (128, 2), BF16)
    ctx_bf = p.sb("ctx_bf", (128, 128), BF16)
    tscr = p.sb("tscr", (128, 512))
    den_r = p.sb("den_r", (1, 512))
    den_c = p.sb("den_c", (128, 4))
    o_t = p.sb("o_t", (128, 64))
    o_bf = p.sb("o_bf", (128, 64), BF16)
    tout = p.sb("tout", (128, 128), BF16)
    KP = p.ps("KP", (128, 256))
    QP = p.ps("QP", (128, 512))
    RP = p.ps("RP", (1, 256))
    CP = p.ps("CP", (128, 128))
    DEN = p.ps("DEN", (1, 512))
    TPf = p.ps("TPf", (128, 128))
    TPB = p.ps("TPB", (128, 128), BF16)

    p.dma(id_f[:, :], ident[:, :])
    p.v('tensor_copy', id_bf[:, :], id_f[:, :])
    p.v('memset', onesc[:, :], 1.0)
    p.v('memset', ones_f[:, :], 1.0)
    p.dma(pj_sb[:, :], projT[:, :])
    p.v('tensor_copy', pj_bf[:, :], pj_sb[:, :])
    p.dma(mrow[0:1, :], minss8.rearrange("a b -> b a"))
    p.v('tensor_reduce', mrow[0:1, 0:1], mrow[0:1, :], AX.X, ALU.min)
    p.v('tensor_scalar_mul', mrow[0:1, 0:1], mrow[0:1, 0:1], -0.5)
    p.v('tensor_copy', stabc[0:1, 0:1], mrow[0:1, 0:1])
    p.mm(TPf[:, 0:1], ones_f[0:1, :], stabc[0:1, 0:1])
    p.v('tensor_copy', stabc[:, 0:1], TPf[:, 0:1])

    for h in range(4):
        p.dma(q_sb[:, :], qfm[h * 64:(h + 1) * 64, :])
        p.dma(k_sb[:, :], kfm[h * 64:(h + 1) * 64, :])
        p.s('mul', q_sb[:, :], q_sb[:, :], C4)
        p.s('mul', k_sb[:, :], k_sb[:, :], C4)
        for o_ in range(64):
            p.dma(v_sb[:, o_ * 64:(o_ + 1) * 64], vtm[o_ * 128:(o_ + 1) * 128, h * 64:(h + 1) * 64])
        for t in range(64):
            p.mm(KP[:, :], k_sb[:, t * 128:(t + 1) * 128], pj_bf[:, :])
            p.dma(ss_sb[:, 0:1], ssk[t * 128:(t + 1) * 128, h:h + 1], allow_slow_non_contiguous=True)
            p.v('tensor_scalar_mul', bias_c[:, 0:1], ss_sb[:, 0:1], -0.5)
            p.v('tensor_sub', bias_c[:, 0:1], bias_c[:, 0:1], stabc[:, 0:1])
            p.act(tscr[:, 0:256], KP[:, :], AF.Exp, bias=bias_c[:, 0:1])
            p.v('tensor_scalar', kf_tm[:, t * 256:(t + 1) * 256], tscr[:, 0:256],
                DN, DN * EPS, ALU.mult, ALU.add)
            p.mm(RP[0:1, :], onesc[:, :], kf_tm[:, t * 256:(t + 1) * 256],
                 start=(t == 0), stop=(t == 63))
        p.v('tensor_copy', tscr[0:1, 0:256], RP[0:1, :])
        for mc in range(2):
            p.v('tensor_copy', den_r[0:1, 0:128], tscr[0:1, mc * 128:(mc + 1) * 128])
            p.trans(TPf[:, 0:1], den_r[0:1, 0:128], id_f[0:1, 0:1])
            p.v('tensor_copy', den_c[:, 3:4], TPf[:, 0:1])
            p.v('tensor_copy', kcs_bf[:, mc:mc + 1], den_c[:, 3:4])
        for mc in range(2):
            for t in range(64):
                p.mm(CP[:, mc * 64:(mc + 1) * 64], kf_tm[:, t * 256 + mc * 128:t * 256 + mc * 128 + 128],
                     v_sb[:, t * 64:(t + 1) * 64], start=(t == 0), stop=(t == 63))
        p.v('tensor_copy', ctx_bf[:, :], CP[:, :])
        for mc in range(2):
            for nc_ in range(16):
                p.mm(QP[:, :], pj_bf[:, mc * 128:(mc + 1) * 128],
                     q_sb[:, nc_ * 512:(nc_ + 1) * 512])
                p.act(tscr[:, :], QP[:, :], AF.Exp)
                p.v('tensor_scalar', qf_fm[:, mc * SEQ + nc_ * 512:mc * SEQ + (nc_ + 1) * 512],
                    tscr[:, :], DN, DN * EPS, ALU.mult, ALU.add)
        for nc_ in range(16):
            for mc in range(2):
                p.mm(DEN[0:1, :], kcs_bf[:, mc:mc + 1],
                     qf_fm[:, mc * SEQ + nc_ * 512:mc * SEQ + (nc_ + 1) * 512],
                     start=(mc == 0), stop=(mc == 1))
            p.v('reciprocal', den_r[0:1, :], DEN[0:1, :])
            for tl in range(4):
                t = nc_ * 4 + tl
                p.v('tensor_copy', tscr[0:1, 0:128], den_r[0:1, tl * 128:(tl + 1) * 128])
                p.trans(TPf[:, 0:1], tscr[0:1, 0:128], id_f[0:1, 0:1])
                p.v('tensor_copy', den_c[:, 0:1], TPf[:, 0:1])
                for mc in range(2):
                    p.mm(QP[:, 0:64],
                         qf_fm[:, mc * SEQ + t * 128:mc * SEQ + (t + 1) * 128],
                         ctx_bf[:, mc * 64:(mc + 1) * 64], start=(mc == 0), stop=(mc == 1))
                p.v('tensor_scalar_mul', o_t[:, 0:64], QP[:, 0:64], den_c[:, 0:1])
                p.v('tensor_copy', o_bf[:, 0:64], o_t[:, 0:64])
                p.trans(TPB[0:64, 0:128], o_bf[:, 0:64], id_bf[:, :])
                p.v('tensor_copy', tout[0:64, 0:128], TPB[0:64, 0:128])
                p.dma(ofm[h * 64:(h + 1) * 64, t * 128:(t + 1) * 128], tout[0:64, 0:128])
    return p


def block_performer(x, ins):
    perm = np.arange(1024)
    WqT = f32(ins['L1_attn_b_Wq']).T
    WkT = f32(ins['L1_attn_b_Wk']).T
    WvT = f32(ins['L1_attn_b_Wv']).T
    Wqkv = np.concatenate([WqT, WkT, WvT], axis=1)
    bias = np.concatenate([f32(ins['L1_attn_b_bq']), f32(ins['L1_attn_b_bk']),
                           f32(ins['L1_attn_b_bv'])])
    qfm, kfm, v, ssk, minss = run_PA(x, Wqkv, bias, qmul=1.0, perf=True)
    projT = f32(ins['L1_attn_b_proj']).T
    ident = np.eye(128, dtype=np.float32)
    im = []
    for c in range(NCORE):
        b_, hg = c // 4, c % 4
        rs = slice(hg * 256, (hg + 1) * 256)
        cs = slice(b_ * SEQ, (b_ + 1) * SEQ)
        im.append({"qfm": np.ascontiguousarray(qfm[rs, cs]),
                   "kfm": np.ascontiguousarray(kfm[rs, cs]),
                   "vtm": np.ascontiguousarray(v[cs, rs]),
                   "ssk": f32(ssk[c // 4 * SEQ:(c // 4 + 1) * SEQ, hg * 4:(hg + 1) * 4]),
                   "minss8": f32(minss), "projT": projT, "ident": ident})
    res = runp(build_P7b, ("P7b",), im)
    Ofm = np.zeros((D, BT), dtype=ml_dtypes.bfloat16)
    for c in range(NCORE):
        b_, hg = c // 4, c % 4
        Ofm[hg * 256:(hg + 1) * 256, b_ * SEQ:(b_ + 1) * SEQ] = res[c]["ofm"]
    return run_PO(x, Ofm, f32(ins['L1_attn_b_Wo']).T, f32(ins['L1_attn_b_bo']), add_ln=False)



def kernel(**ins):
    x = ins['L0_attn_a_hidden_states'].astype(np.float32).reshape(BT, D)
    x = block_deltanet(x, ins)
    x = block_l0ffn(x, ins)
    x = block_attn_b(x, ins)
    x = block_l0ffn(x, ins)
    x = block_attn_a_L1(x, ins)
    x = block_moe(x, ins, 'L1_ffn_')
    x = block_performer(x, ins)
    x = block_moe(x, ins, 'L1_ffn_')
    return x.reshape(2, SEQ, D).astype(np.float32)
